# version 22
# baseline (speedup 1.0000x reference)
import numpy as np
import ml_dtypes
from contextlib import ExitStack

import concourse.bass as bass
import concourse.mybir as mybir
from concourse.bass_utils import run_bass_kernel_spmd

F32 = mybir.dt.float32
BF16 = mybir.dt.bfloat16
I32 = mybir.dt.int32
AF = mybir.ActivationFunctionType
ALU = mybir.AluOpType
AX = mybir.AxisListType

T = 2048
D = 1024
H = 16
HD = 64
FF = 4096
NT = T // 128
NJ = D // 128
NF = FF // 128
EPS_LN = 1e-5
EPS_ATT = 1e-8
N_CORES = 8

PHASES = 99
N_PREP = 4
DEFER_TAIL = True
DBG = False


class Sem:
    def __init__(self, stack, nc, name):
        self.h = stack.enter_context(nc.semaphore(name=name))
        self.n = 0


class Plan:
    ENGINES = ("sync", "gpsimd", "tensor", "vector", "scalar")

    def __init__(self, nc, stack):
        self.nc = nc
        self.ops = {e: [] for e in self.ENGINES}
        self.sT = Sem(stack, nc, "sT")
        self.sV = Sem(stack, nc, "sV")
        self.sS = Sem(stack, nc, "sS")

    def add(self, eng, fn):
        self.ops[eng].append(fn)

    def wait(self, eng, sem, val):
        if val > 0:
            self.add(eng, lambda e, s=sem, v=val: e.wait_ge(s.h, v))

    def op(self, eng, fn, sem=None, amount=1):
        if sem is None:
            self.add(eng, fn)
            return None
        sem.n += amount
        val = sem.n
        def run(e, f=fn, s=sem, a=amount):
            f(e).then_inc(s.h, a)
        self.add(eng, run)
        return val

    def emit(self, block):
        for name in self.ENGINES:
            ops = self.ops[name]
            def body(e, ops=ops):
                for f in ops:
                    f(e)
            getattr(block, name)(body)


class Arena:
    def __init__(self, nc, stack, nbytes):
        self.nbytes = nbytes
        self.t = stack.enter_context(nc.sbuf_tensor("arena", [128, nbytes], mybir.dt.uint8))

    def view(self, off, n_elems, dtype):
        esz = mybir.dt.size(dtype)
        assert off % 64 == 0, off
        assert off + n_elems * esz <= self.nbytes, (off, n_elems, esz)
        return self.t[:, off:off + n_elems * esz].bitcast(dtype)


def build_program():
    nc = bass.Bass("TRN2", target_bir_lowering=False)
    dbg_out = {}

    x_h = nc.dram_tensor("x", [T, D], F32, kind="ExternalInput").ap()
    wq_h = nc.dram_tensor("wq", [D, D], F32, kind="ExternalInput").ap()
    wk_h = nc.dram_tensor("wk", [D, D], F32, kind="ExternalInput").ap()
    wv_h = nc.dram_tensor("wv", [D, D], F32, kind="ExternalInput").ap()
    wo_h = nc.dram_tensor("wo", [D, D], F32, kind="ExternalInput").ap()
    bo_h = nc.dram_tensor("bo", [1, D], F32, kind="ExternalInput").ap()
    w1_h = nc.dram_tensor("w1", [FF, D], F32, kind="ExternalInput").ap()
    b1_h = nc.dram_tensor("b1", [1, FF], F32, kind="ExternalInput").ap()
    w2_h = nc.dram_tensor("w2", [D, FF], F32, kind="ExternalInput").ap()
    b2_h = nc.dram_tensor("b2", [1, D], F32, kind="ExternalInput").ap()
    ident_h = nc.dram_tensor("ident", [128, 128], F32, kind="ExternalInput").ap()
    identb_h = nc.dram_tensor("identb", [128, 128], BF16, kind="ExternalInput").ap()
    ones_h = nc.dram_tensor("ones", [128, 128], F32, kind="ExternalInput").ap()
    eshift_h = nc.dram_tensor("eshift", [64, 128], BF16, kind="ExternalInput").ap()
    out_h = nc.dram_tensor("out", [T, D], F32, kind="ExternalOutput").ap()

    wkT_s = nc.dram_tensor("wkT_s", [NJ, 128, NJ, 128], BF16, kind="Internal").ap()
    wqT_s = nc.dram_tensor("wqT_s", [NJ, 128, NJ, 128], BF16, kind="Internal").ap()
    woT_s = nc.dram_tensor("woT_s", [NJ, 128, NJ, 128], BF16, kind="Internal").ap()
    w1t_s = nc.dram_tensor("w1t_s", [FF, D], BF16, kind="Internal").ap()
    w1tT_s = nc.dram_tensor("w1tT_s", [D, FF], BF16, kind="Internal").ap()
    w2t_s = nc.dram_tensor("w2t_s", [D, FF], BF16, kind="Internal").ap()
    w2T2_s = nc.dram_tensor("w2T2_s", [FF, D], BF16, kind="Internal").ap()

    stack = ExitStack()
    with stack:
        A = Arena(nc, stack, 204 * 1024)
        P = Plan(nc, stack)
        sT, sV, sS = P.sT, P.sV, P.sS

        pA = stack.enter_context(nc.psum_tensor("pA", [128, 1024], F32))
        pB = stack.enter_context(nc.psum_tensor("pB", [128, 1024], F32))
        pC = stack.enter_context(nc.psum_tensor("pC", [128, 1024], F32))
        pD = stack.enter_context(nc.psum_tensor("pD", [128, 1024], F32))

        off = [0]
        def alloc(nbytes):
            o = off[0]
            off[0] += (nbytes + 63) // 64 * 64
            return o
        o_identb = alloc(128 * 2)
        o_ident = alloc(128 * 4)
        o_ones = alloc(128 * 4)
        o_eshift = alloc(128 * 2)
        o_rows = alloc(4096)
        o_b1c = alloc(32 * 4)
        o_thr = alloc(4 * 4)
        o_onesb = alloc(128 * 2)
        o_stats = alloc(128 * 4)
        o_x = alloc(2 * 1024 * 4)
        o_M1 = alloc(97 * 1024)
        o_M2 = alloc(78 * 1024)
        o_M3 = alloc(14 * 1024)
        assert off[0] <= A.nbytes, off[0]

        identb = A.view(o_identb, 128, BF16)
        ident = A.view(o_ident, 128, F32)
        ones = A.view(o_ones, 128, F32)
        eshift = A.view(o_eshift, 128, BF16)
        bo_row = A.view(o_rows, 1024, F32)
        b2_row = A.view(o_rows, 1024, F32)
        b1c = A.view(o_b1c, 32, F32)
        thrc = A.view(o_thr, 4, F32)
        onesb = A.view(o_onesb, 128, BF16)
        stats = A.view(o_stats, 128, F32)
        xst = A.view(o_x, 2048, F32)

        kT = A.view(o_M1, NJ * 2048, BF16)
        vp = A.view(o_M1 + 32 * 1024, 16 * 1040, BF16)
        o_qa = o_M1 + 32 * 1024 + (16 * 1040 * 2 + 63) // 64 * 64
        qa = A.view(o_qa, NJ * 2048, BF16)
        woT = A.view(o_M1, NJ * 1024, BF16)
        G3 = A.view(o_M1, NF * 512, BF16)
        y3T = A.view(o_M1 + 32 * 1024, NJ * 2048, BF16)
        o_p3 = o_qa
        w1slab = A.view(o_p3, 2 * 8 * 256, BF16)
        w2buf = A.view(o_p3 + 8 * 1024, 2 * 1024, BF16)
        a2 = A.view(o_p3 + 12 * 1024, 4 * 1024, F32)
        assert o_p3 + 28 * 1024 <= o_M1 + 97 * 1024

        y1T = A.view(o_M2, NJ * 2048, BF16)
        wvT = A.view(o_M2 + 32 * 1024, NJ * 1024, BF16)
        wstage = A.view(o_M2 + 48 * 1024, 4 * 1024, BF16)
        p1buf = A.view(o_M2 + 56 * 1024, 4 * 1024, BF16)
        evactmp = A.view(o_M2 + 64 * 1024, 1024, BF16)
        wslab = A.view(o_M2 + 66 * 1024, 2 * 8 * 128, BF16)
        b1v = A.view(o_M2 + 70 * 1024, 128, F32)
        X1 = A.view(o_M2, NT * 1024, F32)
        o_sp = o_M2 + 64 * 1024
        p2chunk = A.view(o_sp, 2 * 1024, F32)
        p2g = A.view(o_sp + 8 * 1024, 1024, BF16)
        p2l = A.view(o_sp + 10 * 1024, 1024, BF16)
        p2t = A.view(o_sp + 12 * 1024, 1024, BF16)
        convt = A.view(o_sp, 4096, BF16)

        e_b = A.view(o_M2 + 32 * 1024, 16 * 1024, BF16)
        y1buf = A.view(o_M3, 2 * 1024, BF16)
        num_sb = A.view(o_M3, 1024, F32)
        num_sb1 = A.view(o_M3 + 4 * 1024, 1024, F32)
        dbuf = A.view(o_M3 + 4 * 1024, 1024, F32)
        rowbuf = A.view(o_M3 + 8 * 1024, 1024, F32)
        denrow = A.view(o_x, 2048, F32)
        rb_hi = A.view(o_M3, 2048, BF16)
        rb_lo = A.view(o_M3 + 4 * 1024, 2048, BF16)
        bo_full = A.view(o_M3 + 8 * 1024, 1024, F32)
        b2_full = A.view(o_sp, 1024, F32)
        attn_tmp = A.view(o_M3 + 12 * 1024, 1024, BF16)
        sqbuf = A.view(o_M3 + 12 * 1024, 1024, BF16)

        def kT_c(j):
            return kT[:, j * 2048:(j + 1) * 2048]
        def qa_c(j):
            return qa[:, j * 2048:(j + 1) * 2048]
        def y1T_c(j):
            return y1T[:, j * 2048:(j + 1) * 2048]
        def y3T_c(j):
            return y3T[:, j * 2048:(j + 1) * 2048]
        def wvT_c(j):
            return wvT[:, j * 1024:(j + 1) * 1024]
        def woT_c(j):
            return woT[:, j * 1024:(j + 1) * 1024]
        def G_c(f):
            return G3[:, f * 512:(f + 1) * 512]
        def xst_t(i):
            return xst[:, i * 1024:(i + 1) * 1024]
        def a2_t(i):
            return a2[:, i * 1024:(i + 1) * 1024]
        def eb(hh, s):
            i = hh * 8 + (s % 8)
            return e_b[:, i * 1024:(i + 1) * 1024]
        def ybuf(i):
            return y1buf[:, (i % 2) * 1024:((i % 2) + 1) * 1024]
        def X1_t(tt):
            return X1[:, tt * 1024:(tt + 1) * 1024]
        def vp_head(s, h):
            return vp[:, s * 1040 + 65 * h: s * 1040 + 65 * h + 65]
        def stc(i):
            return stats[:, i:i + 1]

        ST_S1, ST_S2, ST_MEAN, ST_RSTD, ST_NMR, ST_LN, ST_ABS, ST_EPS, ST_TMP = \
            0, 1, 2, 3, 4, 5, 6, 7, 8
        ST_BC = 9

        def dsem(name):
            return Sem(stack, nc, name)
        sCst = dsem("sCst")
        sStage = dsem("sStage")
        sPrepS = dsem("sPrepS")
        sXa = [dsem("sXa0"), dsem("sXa1")]
        sKQ = [dsem("sKQ0"), dsem("sKQ1")]
        sP1L = [dsem("sP1L0"), dsem("sP1L1"), dsem("sP1L2"), dsem("sP1L3")]
        sP2L = [dsem("sP2L0"), dsem("sP2L1")]
        sP2S = dsem("sP2S")
        sCnvL = dsem("sCnvL")
        sCnvS = dsem("sCnvS")
        sWoL = dsem("sWoL")
        sX2 = [dsem("sX20"), dsem("sX21")]
        sW1 = [dsem("sW10"), dsem("sW11")]
        sW2 = [dsem("sW20"), dsem("sW21")]
        sO = [dsem("sO0"), dsem("sO1")]

        def vjunk():
            P.op("vector", lambda e: e.tensor_copy(
                out=sqbuf[:, 768:1024], in_=sqbuf[:, 512:768]))

        def vjunk_tk():
            return P.op("vector", lambda e: e.tensor_copy(
                out=sqbuf[:, 768:1024], in_=sqbuf[:, 512:768]), sV)

        stats_guard = [0]

        def ln_stats(src_ap, tk_src=None, src_sem=None):
            if tk_src is not None:
                P.wait("scalar", src_sem, tk_src)
            P.wait("scalar", sV, stats_guard[0])
            P.op("scalar", lambda e: e.activation(
                out=sqbuf[:], in_=src_ap, func=AF.Copy, accum_out=stc(ST_S1)))
            P.op("scalar", lambda e: e.activation(
                out=sqbuf[:], in_=src_ap, func=AF.Square, accum_out=stc(ST_S2)))
            tk_s12 = P.op("scalar", lambda e: e.activation(
                out=stats[:, 44:48], in_=stats[:, 44:48], func=AF.Copy), sS)
            P.wait("vector", sS, tk_s12)
            P.op("vector", lambda e: e.tensor_scalar(
                out=stc(ST_LN), in0=stc(ST_S1), scalar1=stc(ST_S1), scalar2=-1.0 / (D * D),
                op0=ALU.mult, op1=ALU.mult))
            stats_guard[0] = P.op("vector", lambda e: e.tensor_scalar(
                out=stc(ST_MEAN), in0=stc(ST_S1), scalar1=1.0 / D, scalar2=None,
                op0=ALU.mult), sV)
            vjunk()
            P.op("vector", lambda e: e.tensor_scalar(
                out=stc(ST_BC), in0=stc(ST_LN), scalar1=EPS_LN, scalar2=None,
                op0=ALU.add))
            vjunk()
            tkb = vjunk_tk()
            P.wait("scalar", sV, tkb)
            P.op("scalar", lambda e: (
                e.activation(out=stc(ST_LN), in_=stc(ST_S2), func=AF.Ln,
                             scale=1.0 / D, bias=stc(ST_BC)),
                e.activation(out=stc(ST_RSTD), in_=stc(ST_LN), func=AF.Exp,
                             scale=-0.5))[-1])
            tk_r = P.op("scalar", lambda e: e.activation(
                out=stats[:, 44:48], in_=stats[:, 44:48], func=AF.Copy), sS)
            P.wait("vector", sS, tk_r)
            P.op("vector", lambda e: e.tensor_scalar(
                out=stc(ST_NMR), in0=stc(ST_MEAN), scalar1=stc(ST_RSTD), scalar2=-1.0,
                op0=ALU.mult, op1=ALU.mult))
            vjunk()
            vjunk()
            return tk_r

        P.op("sync", lambda e: e.dma_start(out=identb[:], in_=identb_h), sCst, 16)
        P.op("sync", lambda e: e.dma_start(out=ident[:], in_=ident_h), sCst, 16)
        P.op("sync", lambda e: e.dma_start(out=ones[:], in_=ones_h), sCst, 16)
        P.op("sync", lambda e: e.dma_start(out=eshift[0:64, :], in_=eshift_h), sCst, 16)
        P.op("sync", lambda e: e.dma_start(out=bo_row[0:1, :], in_=bo_h), sCst, 16)
        P.op("sync", lambda e: e.dma_start(out=b2_row[32:33, :], in_=b2_h), sCst, 16)
        P.op("sync", lambda e: e.dma_start(
            out=b1v[0:32, :], in_=b1_h.rearrange("o (a b) -> (o a) b", b=128)), sCst, 16)
        tk_consts = sCst.n

        P.wait("tensor", sCst, tk_consts)
        tkb1 = P.op("tensor", lambda e: e.matmul(
            pD[:, 0:32], b1v[0:32, :], ident[0:32, 0:32], start=True, stop=True), sT)
        P.wait("vector", sT, tkb1)
        P.op("vector", lambda e: e.tensor_copy(out=b1c[:], in_=pD[:, 0:32]))
        P.op("vector", lambda e: e.memset(onesb[64:65, :], 1.0))
        tk_init_v = P.op("vector", lambda e: e.memset(vp[:], 1.0), sV)

        if PHASES < 1:
            return _finish(nc, stack, P, A, dbg_out, locals())

        prep_order = [("wk", wk_h, wkT_s), ("wv", wv_h, None),
                      ("wq", wq_h, wqT_s), ("wo", wo_h, woT_s)][:N_PREP]
        evac_tks = [tk_init_v, tk_init_v]
        wprep_done_v = {}
        for wname, w_h, w_scratch in prep_order:
            for half in range(2):
                P.wait("gpsimd", sV, evac_tks[-1])
                tk_st = P.op("gpsimd", lambda e, w_h=w_h, half=half: e.dma_start(
                    out=wstage[:].rearrange("p (r j) -> p r j", r=4),
                    in_=w_h.rearrange("(r p) j -> p r j", p=128)[:, half * 4:half * 4 + 4, :]),
                    sStage, 16)
                for r4 in range(4):
                    rw = half * 4 + r4
                    ps = pA if rw % 2 == 0 else pD
                    P.wait("tensor", sStage, tk_st)
                    P.wait("tensor", sV, evac_tks[-2])
                    for cw in range(NJ):
                        P.op("tensor", lambda e, r4=r4, cw=cw, ps=ps: e.matmul(
                            ps[:, cw * 128:(cw + 1) * 128],
                            wstage[:].rearrange("p (r j) -> p r j", r=4)[:, r4, cw * 128:(cw + 1) * 128],
                            identb[:], start=True, stop=True),
                            sT if cw == NJ - 1 else None)
                    tk_mm = sT.n
                    P.wait("vector", sT, tk_mm)
                    if wname == "wv":
                        dv = wvT.rearrange("p (c r i) -> p c r i", c=8, r=8)
                        tk_ev = P.op("vector", lambda e, dv=dv, rw=rw, ps=ps: e.tensor_copy(
                            out=dv[:, :, rw, :],
                            in_=ps[:].rearrange("p (c i) -> p c i", c=8)), sV)
                    else:
                        P.wait("vector", sPrepS, sPrepS.n)
                        tk_ev = P.op("vector", lambda e, ps=ps: e.tensor_copy(
                            out=evactmp[:], in_=ps[:]), sV)
                        P.wait("sync", sV, tk_ev)
                        P.op("sync", lambda e, w_scratch=w_scratch, rw=rw: e.dma_start(
                            out=w_scratch[rw],
                            in_=evactmp[:].rearrange("p (c i) -> p c i", c=8)), sPrepS, 16)
                    evac_tks.append(tk_ev)
            wprep_done_v[wname] = evac_tks[-1]
        wprep_stores_all = sPrepS.n

        if PHASES < 2:
            return _finish(nc, stack, P, A, dbg_out, locals())

        pass1_jobs = []
        for wi, w_h in enumerate((w1_h, w2_h)):
            flat = (w_h.rearrange("(c p) j -> c p j", p=128) if w_h.shape[0] == FF
                    else w_h.rearrange("d (c j) -> (d c) j", j=1024).rearrange(
                        "(c p) j -> c p j", p=128))
            for c in range(32):
                pass1_jobs.append((flat, c))
        p1_free_v = [0, 0, 0, 0]
        p1_idx = [0]

        def emit_pass1_job():
            idx = p1_idx[0]
            if idx >= len(pass1_jobs):
                return
            p1_idx[0] += 1
            flat, c = pass1_jobs[idx]
            buf = idx % 4
            P.wait("gpsimd", sV, p1_free_v[buf])
            tk_l = P.op("gpsimd", lambda e, flat=flat, c=c, buf=buf: e.dma_start(
                out=p1buf[:, buf * 1024:(buf + 1) * 1024], in_=flat[c]), sP1L[buf], 16)
            P.wait("vector", sP1L[buf], tk_l)
            tk_a = P.op("vector", lambda e, buf=buf, idx=idx: e.reduce_sum(
                out=stats[:, 64 + idx:65 + idx],
                in_=p1buf[:, buf * 1024:(buf + 1) * 1024],
                axis=AX.X, apply_absolute_value=True), sV)
            p1_free_v[buf] = tk_a

        x_free_v = [0, 0]
        ybuf_free_t = [0, 0]
        y1_tks = []
        vput_tks = []
        for tt in range(NT):
            buf = tt % 2
            P.wait("sync", sV, x_free_v[buf])
            tk_x = P.op("sync", lambda e, tt=tt, buf=buf: e.dma_start(
                out=xst_t(buf), in_=x_h[tt * 128:(tt + 1) * 128, :]), sXa[buf], 16)
            P.wait("vector", sXa[buf], tk_x)
            ln_stats(xst_t(buf), tk_x, sXa[buf])
            P.wait("vector", sT, ybuf_free_t[buf])
            tk_y1 = P.op("vector", lambda e, buf=buf: e.tensor_scalar(
                out=ybuf(buf), in0=xst_t(buf),
                scalar1=stc(ST_RSTD), scalar2=stc(ST_NMR), op0=ALU.mult, op1=ALU.add), sV)
            x_free_v[buf] = tk_y1
            psy = pB if tt % 2 == 0 else pC
            P.wait("tensor", sV, tk_y1)
            if tt >= 2:
                P.wait("tensor", sV, y1_tks[tt - 2])
            for c in range(NJ):
                P.op("tensor", lambda e, c=c, buf=buf, psy=psy: e.matmul(
                    psy[:, c * 128:(c + 1) * 128],
                    ybuf(buf)[:, c * 128:(c + 1) * 128],
                    identb[:], start=True, stop=True),
                    sT if c == NJ - 1 else None)
            ybuf_free_t[buf] = sT.n
            P.wait("vector", sT, sT.n)
            tk_ev = P.op("vector", lambda e, tt=tt, psy=psy: e.tensor_copy(
                out=y1T[:].rearrange("p (c r) -> p c r", c=NJ)[:, :, tt * 128:(tt + 1) * 128],
                in_=psy[:].rearrange("p (c r) -> p c r", c=NJ)), sV)
            y1_tks.append(tk_ev)
            psv = pA if tt % 2 == 0 else pD
            P.wait("tensor", sV, tk_ev)
            if tt >= 2:
                P.wait("tensor", sV, vput_tks[tt - 2])
            if tt == 0:
                P.wait("tensor", sV, wprep_done_v["wv"])
            for j in range(NJ):
                for nh in range(2):
                    P.op("tensor", lambda e, j=j, nh=nh, tt=tt, psv=psv: e.matmul(
                        psv[:, nh * 512:(nh + 1) * 512],
                        y1T_c(j)[:, tt * 128:(tt + 1) * 128],
                        wvT_c(j)[:, nh * 512:(nh + 1) * 512],
                        start=(j == 0), stop=(j == NJ - 1)),
                        sT if (j == NJ - 1 and nh == 1) else None)
            P.wait("vector", sT, sT.n)
            tk_vev = P.op("vector", lambda e, tt=tt, psv=psv: e.tensor_copy(
                out=vp[:, tt * 1040:(tt + 1) * 1040].rearrange(
                    "p (h q) -> p h q", h=16)[:, :, 0:64],
                in_=psv[:].rearrange("p (h q) -> p h q", h=16)), sV)
            vput_tks.append(tk_vev)
            for _ in range(4):
                emit_pass1_job()

        slab_free_v = [0, 0]
        kq_last_v = 0
        for wname, w_scr, dst_c in (("wk", wkT_s, kT_c), ("wq", wqT_s, qa_c)):
            for ic in range(NJ):
                buf = ic % 2
                P.wait("sync", sV, slab_free_v[buf])
                P.wait("sync", sPrepS, wprep_stores_all)
                tk_sl = P.op("sync", lambda e, w_scr=w_scr, ic=ic, buf=buf: e.dma_start(
                    out=wslab[:, buf * 1024:(buf + 1) * 1024].rearrange("p (c i) -> p c i", c=8),
                    in_=w_scr[ic]), sKQ[buf], 16)
                P.wait("tensor", sKQ[buf], tk_sl)
                P.wait("tensor", sV, kq_last_v)
                for j in range(NJ):
                    for qc in range(2):
                        ps = pB if qc == 0 else pC
                        for nh in range(2):
                            P.op("tensor", lambda e, j=j, qc=qc, nh=nh, buf=buf, ps=ps: e.matmul(
                                ps[:, nh * 512:(nh + 1) * 512],
                                wslab[:, buf * 1024 + j * 128:buf * 1024 + (j + 1) * 128],
                                y1T_c(j)[:, qc * 1024 + nh * 512:qc * 1024 + (nh + 1) * 512],
                                start=(j == 0), stop=(j == NJ - 1)),
                                sT if (j == NJ - 1 and qc == 1 and nh == 1) else None)
                P.wait("vector", sT, sT.n)
                P.op("vector", lambda e, dst_c=dst_c, ic=ic: e.tensor_copy(
                    out=dst_c(ic)[:, 0:1024], in_=pB[:]))
                tk_e2 = P.op("vector", lambda e, dst_c=dst_c, ic=ic: e.tensor_copy(
                    out=dst_c(ic)[:, 1024:2048], in_=pC[:]), sV)
                slab_free_v[buf] = tk_e2
                kq_last_v = tk_e2

        while p1_idx[0] < len(pass1_jobs):
            emit_pass1_job()
        P.op("vector", lambda e: e.tensor_copy(out=sqbuf[:, 256:512], in_=sqbuf[:, 0:256]))
        tk_abs = P.op("vector", lambda e: e.tensor_copy(
            out=sqbuf[:, 256:512], in_=sqbuf[:, 0:256]), sV)
        P.wait("tensor", sV, tk_abs)
        tk_red = P.op("tensor", lambda e: e.matmul(
            pD[0:1, 0:64], ones[:, 0:1], stats[:, 64:128], start=True, stop=True), sT)
        P.wait("vector", sT, tk_red)
        P.op("vector", lambda e: e.tensor_copy(out=rowbuf[0:1, 64:128], in_=pD[0:1, 0:64]))
        vjunk()
        vjunk()
        P.op("vector", lambda e: e.reduce_sum(
            out=rowbuf[0:1, 128:129], in_=rowbuf[0:1, 64:96], axis=AX.X))
        P.op("vector", lambda e: e.reduce_sum(
            out=rowbuf[0:1, 129:130], in_=rowbuf[0:1, 96:128], axis=AX.X))
        vjunk()
        vjunk()
        P.op("vector", lambda e: e.tensor_scalar(
            out=rowbuf[0:1, 0:1], in0=rowbuf[0:1, 128:129], scalar1=0.5 / (FF * D),
            scalar2=0.5e-8, op0=ALU.mult, op1=ALU.max))
        P.op("vector", lambda e: e.tensor_scalar(
            out=rowbuf[0:1, 1:2], in0=rowbuf[0:1, 128:129], scalar1=-0.5 / (FF * D),
            scalar2=-0.5e-8, op0=ALU.mult, op1=ALU.min))
        P.op("vector", lambda e: e.tensor_scalar(
            out=rowbuf[0:1, 2:3], in0=rowbuf[0:1, 129:130], scalar1=0.5 / (FF * D),
            scalar2=0.5e-8, op0=ALU.mult, op1=ALU.max))
        P.op("vector", lambda e: e.tensor_scalar(
            out=rowbuf[0:1, 3:4], in0=rowbuf[0:1, 129:130], scalar1=-0.5 / (FF * D),
            scalar2=-0.5e-8, op0=ALU.mult, op1=ALU.min))
        vjunk()
        tk_thr = vjunk_tk()
        P.wait("tensor", sV, tk_thr)
        for ci in range(4):
            P.op("tensor", lambda e, ci=ci: e.matmul(
                pD[:, ci * 16:ci * 16 + 1], ones[0:1, :], rowbuf[0:1, ci:ci + 1],
                start=(ci == 0), stop=(ci == 3)),
                sT if ci == 3 else None)
        P.wait("vector", sT, sT.n)
        tk_thc = P.op("vector", lambda e: e.tensor_copy(
            out=thrc[:], in_=pD[:, 0:64].rearrange("p (a b) -> p a b", b=16)[:, :, 0]), sV)

        if PHASES < 3:
            return _finish(nc, stack, P, A, dbg_out, locals())

        pass2_jobs = []
        for wi, (w_h, w_scr) in enumerate(((w1_h, w1t_s), (w2_h, w2t_s))):
            src = (w_h.rearrange("(c p) j -> c p j", p=128) if w_h.shape[0] == FF
                   else w_h.rearrange("d (c j) -> (d c) j", j=1024).rearrange(
                       "(c p) j -> c p j", p=128))
            dst = (w_scr.rearrange("(c p) j -> c p j", p=128) if w_scr.shape[0] == FF
                   else w_scr.rearrange("d (c j) -> (d c) j", j=1024).rearrange(
                       "(c p) j -> c p j", p=128))
            for c in range(32):
                pass2_jobs.append((src, dst, c, wi))
        p2_free_v = [kq_last_v, kq_last_v]
        p2_prev_store = [0]

        def emit_pass2_job(idx):
            src, dst, c, wi = pass2_jobs[idx]
            buf = idx % 2
            P.wait("sync", sV, p2_free_v[buf])
            tk_l = P.op("sync", lambda e, src=src, c=c, buf=buf: e.dma_start(
                out=p2chunk[:, buf * 1024:(buf + 1) * 1024], in_=src[c]), sP2L[buf], 16)
            P.wait("vector", sP2L[buf], tk_l)
            if idx == 0:
                P.wait("vector", sV, tk_thc)
            cv = p2chunk[:, buf * 1024:(buf + 1) * 1024]
            P.op("vector", lambda e, cv=cv, wi=wi: e.tensor_scalar(
                out=p2g[:], in0=cv, scalar1=thrc[:, 2 * wi:2 * wi + 1], scalar2=None,
                op0=ALU.is_gt))
            P.op("vector", lambda e, cv=cv, wi=wi: e.tensor_scalar(
                out=p2l[:], in0=cv, scalar1=thrc[:, 2 * wi + 1:2 * wi + 2], scalar2=None,
                op0=ALU.is_lt))
            P.wait("vector", sP2S, p2_prev_store[0])
            tk_t = P.op("vector", lambda e: e.tensor_sub(out=p2t[:], in0=p2g[:], in1=p2l[:]), sV)
            p2_free_v[buf] = tk_t
            P.wait("gpsimd", sV, tk_t)
            P.op("gpsimd", lambda e, dst=dst, c=c: e.dma_start(out=dst[c], in_=p2t[:]), sP2S, 16)
            p2_prev_store[0] = sP2S.n

        P.wait("tensor", sV, kq_last_v)
        av_free_v = 0
        last_exp = {}
        pass2_idx = 0
        tail_rest = [None]
        groups = [(hp, qh) for hp in range(H // 2) for qh in range(2)]

        def make_tail_rest(hp, qh, emit_recip, tk_nm):
            jc = hp
            qb = qh * 1024
            def stage(stage_i):
                if stage_i == -1:
                    stage.tk_rc = emit_recip()
                    return
                if stage_i == 0:
                    P.wait("tensor", sV, stage.tk_rc)
                    for hh in range(2):
                        psb = pC if hh == 0 else pD
                        for nh in range(2):
                            c0 = hh * 1024 + nh * 512
                            P.op("tensor", lambda e, psb=psb, nh=nh, c0=c0: e.matmul(
                                psb[:, nh * 512:(nh + 1) * 512], onesb[64:65, :],
                                rb_hi[64:65, c0:c0 + 512], start=True, stop=False))
                            P.op("tensor", lambda e, psb=psb, nh=nh, c0=c0: e.matmul(
                                psb[:, nh * 512:(nh + 1) * 512], onesb[64:65, :],
                                rb_lo[64:65, c0:c0 + 512], start=False, stop=True),
                                sT if (hh == 1 and nh == 1) else None)
                    tk_rb = sT.n
                    P.wait("vector", sT, tk_rb)
                    stage.tk_div0 = P.op("vector", lambda e, jc=jc, qb=qb: e.tensor_tensor(
                        out=qa_c(jc)[0:64, qb:qb + 1024], in0=num_sb[0:64, :],
                        in1=pC[0:64, :], op=ALU.mult), sV)
                    stage.tk_div1 = P.op("vector", lambda e: e.tensor_tensor(
                        out=attn_tmp[0:64, :], in0=num_sb1[0:64, :],
                        in1=pD[0:64, :], op=ALU.mult), sV)
                else:
                    P.wait("tensor", sV, stage.tk_div1)
                    for nh in range(2):
                        P.op("tensor", lambda e, nh=nh: e.matmul(
                            pD[:, nh * 512:(nh + 1) * 512], eshift[0:64, :],
                            attn_tmp[0:64, nh * 512:(nh + 1) * 512], start=True, stop=True),
                            sT if nh == 1 else None)
                    P.wait("vector", sT, sT.n)
                    stage.tk_put = P.op("vector", lambda e, jc=jc, qb=qb: e.tensor_copy(
                        out=qa_c(jc)[64:128, qb:qb + 1024], in_=pD[64:128, :]), sV)
            stage.tk_div0 = None
            stage.tk_div1 = None
            stage.tk_put = None
            return stage

        for g, (hp, qh) in enumerate(groups):
            jc = hp
            h0, h1 = 2 * hp, 2 * hp + 1
            qb = qh * 1024
            exp_tks = {}

            def emit_scores(s, jc=jc, qb=qb, exp_tks=exp_tks):
                for hh in range(2):
                    po = 64 * hh
                    ps = pA if hh == 0 else pB
                    if last_exp.get(hh):
                        P.wait("tensor", sS, last_exp[hh])
                    for nh in range(2):
                        P.op("tensor", lambda e, s=s, nh=nh, hh=hh, ps=ps, jc=jc, po=po, qb=qb: e.matmul(
                            ps[:, nh * 512:(nh + 1) * 512],
                            kT_c(jc)[po:po + 64, s * 128:(s + 1) * 128],
                            qa_c(jc)[po:po + 64, qb + nh * 512:qb + (nh + 1) * 512],
                            start=True, stop=True),
                            sT if nh == 1 else None)
                    P.wait("scalar", sT, sT.n)
                    tk_e = P.op("scalar", lambda e, s=s, hh=hh, ps=ps: e.activation(
                        out=eb(hh, s), in_=ps[:], func=AF.Exp, scale=0.125), sS)
                    last_exp[hh] = tk_e
                    exp_tks[(hh, s)] = tk_e

            def emit_av(s, hp=hp, exp_tks=exp_tks):
                for hh in range(2):
                    h = 2 * hp + hh
                    psav = pC if hh == 0 else pD
                    P.wait("tensor", sS, exp_tks[(hh, s)])
                    if s == 0:
                        P.wait("tensor", sV, av_free_v)
                    for nh in range(2):
                        P.op("tensor", lambda e, s=s, h=h, nh=nh, psav=psav: e.matmul(
                            psav[0:65, nh * 512:(nh + 1) * 512],
                            vp_head(s, h),
                            eb(h % 2, s)[:, nh * 512:(nh + 1) * 512],
                            start=(s == 0), stop=(s == NT - 1)),
                            sT if (s == NT - 1 and nh == 1) else None)

            emit_scores(0)
            if tail_rest[0] is not None:
                tail_rest[0](-1)
            for s in range(1, 6):
                emit_scores(s)
            if tail_rest[0] is not None:
                tail_rest[0](0)
            emit_scores(6)
            if tail_rest[0] is not None:
                tail_rest[0](1)
                av_free_v = tail_rest[0].tk_put
                tail_rest[0] = None
            for _ in range(4):
                if pass2_idx < len(pass2_jobs):
                    emit_pass2_job(pass2_idx)
                    pass2_idx += 1
            for s in range(7, NT):
                emit_scores(s)
                emit_av(s - 7)
            for s in range(NT - 7, NT):
                emit_av(s)
            tk_av = sT.n
            P.wait("vector", sT, tk_av)
            P.op("vector", lambda e: e.tensor_scalar(
                out=denrow[64:65, 0:1024], in0=pC[64:65, 0:1024], scalar1=EPS_ATT,
                scalar2=None, op0=ALU.add))
            tk_dn = P.op("vector", lambda e: e.tensor_scalar(
                out=denrow[64:65, 1024:2048], in0=pD[64:65, 0:1024], scalar1=EPS_ATT,
                scalar2=None, op0=ALU.add), sV)
            P.op("vector", lambda e: e.tensor_copy(
                out=num_sb[0:64, :], in_=pC[0:64, 0:1024]))
            tk_nm = P.op("vector", lambda e: e.tensor_copy(
                out=num_sb1[0:64, :], in_=pD[0:64, 0:1024]), sV)
            def emit_recip(tk_dn=tk_dn):
                P.wait("scalar", sV, tk_dn)
                P.op("scalar", lambda e: e.activation(
                    out=denrow[32:33, 0:2048], in_=denrow[64:65, 0:2048], func=AF.Ln))
                tk_e = P.op("scalar", lambda e: e.activation(
                    out=denrow[64:65, 0:2048], in_=denrow[32:33, 0:2048], func=AF.Exp,
                    scale=-1.0), sS)
                P.wait("vector", sS, tk_e)
                P.op("vector", lambda e: e.tensor_copy(
                    out=rb_hi[64:65, 0:2048], in_=denrow[64:65, 0:2048]))
                return P.op("vector", lambda e: e.tensor_tensor(
                    out=rb_lo[64:65, 0:2048], in0=denrow[64:65, 0:2048],
                    in1=rb_hi[64:65, 0:2048], op=ALU.subtract), sV)
            tail_rest[0] = make_tail_rest(hp, qh, emit_recip, tk_nm)
            if not DEFER_TAIL:
                tail_rest[0](0)
                tail_rest[0](1)
                av_free_v = tail_rest[0].tk_put
                tail_rest[0] = None
        if tail_rest[0] is not None:
            tail_rest[0](-1)
            tail_rest[0](0)
            tail_rest[0](1)
            tail_rest[0] = None
        while pass2_idx < len(pass2_jobs):
            emit_pass2_job(pass2_idx)
            pass2_idx += 1
        attn_done_t = sT.n


        if PHASES < 4:
            return _finish(nc, stack, P, A, dbg_out, locals())

        P.wait("sync", sT, attn_done_t)
        for j in range(NJ):
            P.op("sync", lambda e, j=j: e.dma_start(
                out=woT_c(j).rearrange("p (r i) -> p r i", r=8),
                in_=woT_s[:, :, j, :].rearrange("r p i -> p r i")), sWoL, 16)
        tk_wo = sWoL.n

        P.wait("sync", sP2S, sP2S.n)
        conv_free = 0
        for j in range(NJ):
            P.wait("sync", sCnvS, conv_free)
            tk_c = P.op("sync", lambda e, j=j: e.dma_start(
                out=convt[:], in_=w1t_s[:, j * 128:(j + 1) * 128], transpose=True), sCnvL, 16)
            P.wait("sync", sCnvL, tk_c)
            P.op("sync", lambda e, j=j: e.dma_start(
                out=w1tT_s[j * 128:(j + 1) * 128, :], in_=convt[:]), sCnvS, 16)
            conv_free = sCnvS.n
        NW2 = FF // 128
        LOOK = 3
        tkT2 = [0] * NW2
        tkS2 = [0] * NW2
        for i in range(NW2 + LOOK):
            if i < NW2:
                sl = i % 4
                if i >= 4:
                    P.wait("sync", sCnvS, tkS2[i - 4])
                tkT2[i] = P.op("sync", lambda e, f=i, sl=sl: e.dma_start(
                    out=convt[:, sl * 1024:(sl + 1) * 1024],
                    in_=w2t_s[:, f * 128:(f + 1) * 128], transpose=True), sCnvL, 16)
            k = i - LOOK
            if 0 <= k < NW2:
                P.wait("sync", sCnvL, tkT2[k])
                tkS2[k] = P.op("sync", lambda e, f=k, sl=k % 4: e.dma_start(
                    out=w2T2_s[f * 128:(f + 1) * 128, :],
                    in_=convt[:, sl * 1024:(sl + 1) * 1024]), sCnvS, 16)
        conv_free = sCnvS.n

        attn_v_done = sV.n
        x1_tks = []
        P.wait("tensor", sV, attn_v_done)
        for nh in range(2):
            P.op("tensor", lambda e, nh=nh: e.matmul(
                pC[:, nh * 512:(nh + 1) * 512], ones[0:1, :],
                bo_row[0:1, nh * 512:(nh + 1) * 512], start=True, stop=True),
                sT if nh == 1 else None)
        P.wait("vector", sT, sT.n)
        tk_bo_v = P.op("vector", lambda e: e.tensor_copy(out=bo_full[:], in_=pC[:]), sV)
        x2_free = [attn_v_done, attn_v_done]
        ln2_ps_free = {}
        wo_last_mm = 0
        P.wait("tensor", sWoL, tk_wo)
        for tt in range(NT):
            buf = tt % 2
            P.wait("gpsimd", sV, x2_free[buf])
            tk_x = P.op("gpsimd", lambda e, tt=tt, buf=buf: e.dma_start(
                out=xst_t(buf), in_=x_h[tt * 128:(tt + 1) * 128, :]), sX2[buf], 16)
            ps = (pA, pB, pC, pD)[tt % 4]
            if tt == 2:
                P.wait("tensor", sV, tk_bo_v)
            if tt >= 4:
                P.wait("tensor", sV, ln2_ps_free[tt - 4])
            for j in range(NJ):
                for nh in range(2):
                    P.op("tensor", lambda e, j=j, nh=nh, tt=tt, ps=ps: e.matmul(
                        ps[:, nh * 512:(nh + 1) * 512],
                        qa_c(j)[:, tt * 128:(tt + 1) * 128],
                        woT_c(j)[:, nh * 512:(nh + 1) * 512],
                        start=(j == 0), stop=(j == NJ - 1)),
                        sT if (j == NJ - 1 and nh == 1) else None)
            tk_wmm = sT.n
            wo_last_mm = tk_wmm
            P.wait("vector", sT, tk_wmm)
            tk_d = P.op("vector", lambda e, ps=ps: e.tensor_tensor(
                out=dbuf[:], in0=ps[:], in1=bo_full[:], op=ALU.add), sV)
            ln2_ps_free[tt] = tk_d
            ln_stats(dbuf[:], tk_d, sV)
            P.op("vector", lambda e: e.tensor_scalar(
                out=dbuf[:], in0=dbuf[:], scalar1=stc(ST_RSTD), scalar2=stc(ST_NMR),
                op0=ALU.mult, op1=ALU.add))
            P.wait("vector", sX2[buf], tk_x)
            tk_x1 = P.op("vector", lambda e, tt=tt, buf=buf: e.tensor_tensor(
                out=X1_t(tt), in0=dbuf[:], in1=xst_t(buf), op=ALU.add), sV)
            x2_free[buf] = tk_x1
            x1_tks.append(tk_x1)
        x1_done_v = sV.n

        if PHASES < 5:
            return _finish(nc, stack, P, A, dbg_out, locals())

        y3_tks = []
        y3buf_free_t = [0, 0]
        for tt in range(NT):
            ln_stats(X1_t(tt), x1_tks[tt], sV)
            P.wait("vector", sT, y3buf_free_t[tt % 2])
            tk_y3 = P.op("vector", lambda e, tt=tt: e.tensor_scalar(
                out=ybuf(tt), in0=X1_t(tt), scalar1=stc(ST_RSTD), scalar2=stc(ST_NMR),
                op0=ALU.mult, op1=ALU.add), sV)
            psy = pC if tt % 2 == 0 else pD
            P.wait("tensor", sV, tk_y3)
            if tt >= 2:
                P.wait("tensor", sV, y3_tks[tt - 2])
            for c in range(NJ):
                P.op("tensor", lambda e, c=c, tt=tt, psy=psy: e.matmul(
                    psy[:, c * 128:(c + 1) * 128],
                    ybuf(tt)[:, c * 128:(c + 1) * 128],
                    identb[:], start=True, stop=True),
                    sT if c == NJ - 1 else None)
            y3buf_free_t[tt % 2] = sT.n
            P.wait("vector", sT, sT.n)
            tk_ev = P.op("vector", lambda e, tt=tt, psy=psy: e.tensor_copy(
                out=y3T[:].rearrange("p (c r) -> p c r", c=NJ)[:, :, tt * 128:(tt + 1) * 128],
                in_=psy[:].rearrange("p (c r) -> p c r", c=NJ)), sV)
            y3_tks.append(tk_ev)

        P.wait("tensor", sV, y3_tks[-2])
        for nh in range(2):
            P.op("tensor", lambda e, nh=nh: e.matmul(
                pC[:, nh * 512:(nh + 1) * 512], ones[32:33, :],
                b2_row[32:33, nh * 512:(nh + 1) * 512], start=True, stop=True),
                sT if nh == 1 else None)
        tk_b2mm = sT.n
        P.wait("vector", sT, tk_b2mm)
        P.wait("vector", sCnvS, conv_free)
        tk_bias_v = P.op("vector", lambda e: e.tensor_copy(out=b2_full[:], in_=pC[:]), sV)

        w1_free_t = [wo_last_mm, wo_last_mm]
        w2_free_t = [wo_last_mm, wo_last_mm]
        gelu_tks = {}
        h2_evac_v = {}
        store_cnt = 0
        hps = [pA, pB, pC, pD]
        ln4_tail = [None]

        def make_ln4_tail(qc):
            def tail():
                for tt4 in range(4):
                    P.op("vector", lambda e, tt4=tt4: e.tensor_reduce(
                        out=stats[:, 16 + tt4:17 + tt4], in_=a2_t(tt4), axis=AX.X, op=ALU.add))
                    P.op("vector", lambda e, tt4=tt4: e.scalar_tensor_tensor(
                        out=sqbuf[:], in0=a2_t(tt4), scalar=1.0, in1=a2_t(tt4),
                        op0=ALU.mult, op1=ALU.mult))
                    P.op("vector", lambda e, tt4=tt4: e.tensor_reduce(
                        out=stats[:, 20 + tt4:21 + tt4], in_=sqbuf[:], axis=AX.X, op=ALU.add))
                P.op("vector", lambda e: e.tensor_mul(
                    out=stats[:, 36:40], in0=stats[:, 16:20], in1=stats[:, 16:20]))
                P.op("vector", lambda e: e.tensor_scalar(
                    out=stats[:, 24:28], in0=stats[:, 16:20], scalar1=1.0 / D, scalar2=None,
                    op0=ALU.mult))
                vjunk()
                P.op("vector", lambda e: e.tensor_scalar(
                    out=stats[:, 28:32], in0=stats[:, 36:40], scalar1=-1.0 / (D * D),
                    scalar2=EPS_LN, op0=ALU.mult, op1=ALU.add))
                vjunk()
                tk_v4 = vjunk_tk()
                P.wait("scalar", sV, tk_v4)
                for tt4 in range(4):
                    P.op("scalar", lambda e, tt4=tt4: e.activation(
                        out=stats[:, 36 + tt4:37 + tt4], in_=stats[:, 20 + tt4:21 + tt4],
                        func=AF.Ln, scale=1.0 / D, bias=stats[:, 28 + tt4:29 + tt4]))
                P.op("scalar", lambda e: e.activation(
                    out=stats[:, 32:36], in_=stats[:, 36:40], func=AF.Exp, scale=-0.5))
                tk_r4 = P.op("scalar", lambda e: e.activation(
                    out=stats[:, 44:48], in_=stats[:, 44:48], func=AF.Copy), sS)
                P.wait("vector", sS, tk_r4)
                P.op("vector", lambda e: e.tensor_mul(
                    out=stats[:, 40:44], in0=stats[:, 24:28], in1=stats[:, 32:36]))
                vjunk()
                vjunk()
                for tt4 in range(4):
                    tt = qc * 4 + tt4
                    buf = tt % 2
                    P.wait("vector", sO[buf], sO[buf].n)
                    P.op("vector", lambda e, tt4=tt4: e.tensor_scalar(
                        out=a2_t(tt4), in0=a2_t(tt4), scalar1=stats[:, 32 + tt4:33 + tt4],
                        scalar2=stats[:, 40 + tt4:41 + tt4], op0=ALU.mult, op1=ALU.subtract))
                    tk_o = P.op("vector", lambda e, tt=tt, tt4=tt4, buf=buf: e.tensor_tensor(
                        out=xst_t(buf), in0=a2_t(tt4), in1=X1_t(tt), op=ALU.add), sV)
                    P.wait("gpsimd", sV, tk_o)
                    P.op("gpsimd", lambda e, tt=tt, buf=buf: e.dma_start(
                        out=out_h[tt * 128:(tt + 1) * 128, :], in_=xst_t(buf)), sO[buf], 16)
            return tail

        for qc in range(4):
            for fs in range(16):
                buf = fs % 2
                P.wait("sync", sT, w1_free_t[buf])
                tk_sl = P.op("sync", lambda e, fs=fs, buf=buf: e.dma_start(
                    out=w1slab[:, buf * 2048:(buf + 1) * 2048].rearrange("p (c f) -> p c f", c=8),
                    in_=w1tT_s.rearrange("(c p) f -> p c f", p=128)[:, :, fs * 256:(fs + 1) * 256]),
                    sW1[buf], 16)
                P.wait("tensor", sW1[buf], tk_sl)
                for fh in range(2):
                    f = fs * 2 + fh
                    psh = pA[:, (f % 2) * 512:((f % 2) + 1) * 512]
                    if f >= 2:
                        P.wait("tensor", sS, gelu_tks[(qc, f - 2)])
                    elif qc == 0:
                        P.wait("tensor", sV, y3_tks[-1])
                    else:
                        P.wait("tensor", sV, h2_evac_v[(qc - 1, 0)])
                    for j in range(NJ):
                        P.op("tensor", lambda e, j=j, fh=fh, buf=buf, qc=qc, psh=psh: e.matmul(
                            psh,
                            w1slab[:, buf * 2048 + j * 256 + fh * 128:
                                   buf * 2048 + j * 256 + (fh + 1) * 128],
                            y3T_c(j)[:, qc * 512:(qc + 1) * 512],
                            start=(j == 0), stop=(j == NJ - 1)),
                            sT if j == NJ - 1 else None)
                    tk_h = sT.n
                    if fh == 1:
                        w1_free_t[buf] = tk_h
                    P.wait("scalar", sT, tk_h)
                    gelu_tks[(qc, f)] = P.op("scalar", lambda e, f=f, psh=psh: e.activation(
                        out=G_c(f), in_=psh, func=AF.Gelu, scale=1.0, bias=b1c[:, f:f + 1]), sS)
                if fs == 5 and ln4_tail[0] is not None:
                    ln4_tail[0]()
                    ln4_tail[0] = None
            for f in range(NF):
                buf = f % 2
                P.wait("sync", sT, w2_free_t[buf])
                tk_w2 = P.op("sync", lambda e, f=f, buf=buf: e.dma_start(
                    out=w2buf[:, buf * 1024:(buf + 1) * 1024],
                    in_=w2T2_s[f * 128:(f + 1) * 128, :]), sW2[buf], 16)
                P.wait("tensor", sW2[buf], tk_w2)
                if f == 0:
                    P.wait("tensor", sS, gelu_tks[(qc, NF - 1)])
                    if qc == 0:
                        P.wait("tensor", sV, tk_bias_v)
                    if qc > 0:
                        for tt4 in range(1, 4):
                            P.wait("tensor", sV, h2_evac_v[(qc - 1, tt4)])
                for tt4 in range(4):
                    for nh in range(2):
                        P.op("tensor", lambda e, f=f, tt4=tt4, nh=nh, buf=buf: e.matmul(
                            hps[tt4][:, nh * 512:(nh + 1) * 512],
                            G_c(f)[:, tt4 * 128:(tt4 + 1) * 128],
                            w2buf[:, buf * 1024 + nh * 512:buf * 1024 + (nh + 1) * 512],
                            start=(f == 0), stop=(f == NF - 1)),
                            sT if (tt4 == 3 and nh == 1) else None)
                w2_free_t[buf] = sT.n
            tk_h2 = sT.n
            P.wait("vector", sT, tk_h2)
            for tt4 in range(4):
                h2_evac_v[(qc, tt4)] = P.op("vector", lambda e, tt4=tt4: e.tensor_tensor(
                    out=a2_t(tt4), in0=hps[tt4][:], in1=b2_full[:], op=ALU.add), sV)
            ln4_tail[0] = make_ln4_tail(qc)

        if ln4_tail[0] is not None:
            ln4_tail[0]()
            ln4_tail[0] = None
        P.wait("sync", sO[0], sO[0].n)
        P.wait("sync", sO[1], sO[1].n)
        P.wait("gpsimd", sO[0], sO[0].n)
        P.wait("gpsimd", sO[1], sO[1].n)
        return _finish(nc, stack, P, A, dbg_out, locals())


def _finish(nc, stack, P, A, dbg_out, env):
    sDbg = Sem(stack, nc, "sDbg")
    if DBG:
        views = {
            "y1T": (env.get("y1T"), PHASES >= 2), "kT": (env.get("kT"), 2 <= PHASES < 4),
            "qa": (env.get("qa"), 2 <= PHASES < 5), "vp": (env.get("vp"), 2 <= PHASES < 5),
            "thrc": (env.get("thrc"), PHASES >= 2), "X1": (env.get("X1"), PHASES >= 4),
            "wvT": (env.get("wvT"), 1 <= PHASES < 3),
        }
        for s in (P.sT, P.sV, P.sS):
            P.wait("gpsimd", s, s.n)
        for name, (v, ok) in views.items():
            if not ok or v is None:
                continue
            shape = [128, v.shape[1]]
            d = nc.dram_tensor("dbg_" + name, shape, v.dtype, kind="ExternalOutput").ap()
            P.op("gpsimd", lambda e, d=d, v=v: e.dma_start(out=d, in_=v[:]), sDbg, 16)
        P.wait("gpsimd", sDbg, sDbg.n)
    block = stack.enter_context(nc.Block())
    P.emit(block)
    return nc


_NC_CACHE = {}


def kernel(L0_attn_x, L0_attn_Wq, L0_attn_Wk, L0_attn_Wv, L0_attn_Wo, L0_attn_bo,
           L0_ffn_W1, L0_ffn_b1, L0_ffn_W2, L0_ffn_b2):
    x = np.ascontiguousarray(np.asarray(L0_attn_x, dtype=np.float32))
    wq = np.ascontiguousarray(np.asarray(L0_attn_Wq, dtype=np.float32))
    wk = np.ascontiguousarray(np.asarray(L0_attn_Wk, dtype=np.float32))
    wv = np.ascontiguousarray(np.asarray(L0_attn_Wv, dtype=np.float32))
    wo = np.ascontiguousarray(np.asarray(L0_attn_Wo, dtype=np.float32))
    bo = np.ascontiguousarray(np.asarray(L0_attn_bo, dtype=np.float32)).reshape(1, D)
    w1 = np.ascontiguousarray(np.asarray(L0_ffn_W1, dtype=np.float32))
    b1 = np.ascontiguousarray(np.asarray(L0_ffn_b1, dtype=np.float32)).reshape(1, FF)
    w2 = np.ascontiguousarray(np.asarray(L0_ffn_W2, dtype=np.float32))
    b2 = np.ascontiguousarray(np.asarray(L0_ffn_b2, dtype=np.float32)).reshape(1, D)

    ident = np.eye(128, dtype=np.float32)
    identb = np.eye(128, dtype=ml_dtypes.bfloat16)
    ones = np.ones((128, 128), dtype=np.float32)
    eshift = np.zeros((64, 128), dtype=ml_dtypes.bfloat16)
    for i in range(64):
        eshift[i, 64 + i] = 1.0

    if "nc" not in _NC_CACHE:
        _NC_CACHE["nc"] = build_program()
    nc = _NC_CACHE["nc"]

    in_maps = []
    for c in range(N_CORES):
        in_maps.append({
            "x": np.ascontiguousarray(x[c]),
            "wq": wq, "wk": wk, "wv": wv, "wo": wo, "bo": bo,
            "w1": w1, "b1": b1, "w2": w2, "b2": b2,
            "ident": ident, "identb": identb, "ones": ones,
            "eshift": eshift,
        })
    res = run_bass_kernel_spmd(nc, in_maps, core_ids=list(range(N_CORES)))
    out = np.stack([np.asarray(r["out"]) for r in res.results], axis=0)
    return out.astype(np.float32)


# version 23
# speedup vs baseline: 1.0393x; 1.0393x over previous
import numpy as np
import ml_dtypes
from contextlib import ExitStack

import concourse.bass as bass
import concourse.mybir as mybir
from concourse.bass_utils import run_bass_kernel_spmd

F32 = mybir.dt.float32
BF16 = mybir.dt.bfloat16
I32 = mybir.dt.int32
AF = mybir.ActivationFunctionType
ALU = mybir.AluOpType
AX = mybir.AxisListType

T = 2048
D = 1024
H = 16
HD = 64
FF = 4096
NT = T // 128
NJ = D // 128
NF = FF // 128
EPS_LN = 1e-5
EPS_ATT = 1e-8
N_CORES = 8

PHASES = 99
N_PREP = 4
DEFER_TAIL = True
DBG = False


class Sem:
    def __init__(self, stack, nc, name):
        self.h = stack.enter_context(nc.semaphore(name=name))
        self.n = 0


class Plan:
    ENGINES = ("sync", "gpsimd", "tensor", "vector", "scalar")

    def __init__(self, nc, stack):
        self.nc = nc
        self.ops = {e: [] for e in self.ENGINES}
        self.sT = Sem(stack, nc, "sT")
        self.sV = Sem(stack, nc, "sV")
        self.sS = Sem(stack, nc, "sS")

    def add(self, eng, fn):
        self.ops[eng].append(fn)

    def wait(self, eng, sem, val):
        if val > 0:
            self.add(eng, lambda e, s=sem, v=val: e.wait_ge(s.h, v))

    def op(self, eng, fn, sem=None, amount=1):
        if sem is None:
            self.add(eng, fn)
            return None
        sem.n += amount
        val = sem.n
        def run(e, f=fn, s=sem, a=amount):
            f(e).then_inc(s.h, a)
        self.add(eng, run)
        return val

    def emit(self, block):
        for name in self.ENGINES:
            ops = self.ops[name]
            def body(e, ops=ops):
                for f in ops:
                    f(e)
            getattr(block, name)(body)


class Arena:
    def __init__(self, nc, stack, nbytes):
        self.nbytes = nbytes
        self.t = stack.enter_context(nc.sbuf_tensor("arena", [128, nbytes], mybir.dt.uint8))

    def view(self, off, n_elems, dtype):
        esz = mybir.dt.size(dtype)
        assert off % 64 == 0, off
        assert off + n_elems * esz <= self.nbytes, (off, n_elems, esz)
        return self.t[:, off:off + n_elems * esz].bitcast(dtype)


def build_program():
    nc = bass.Bass("TRN2", target_bir_lowering=False)
    dbg_out = {}

    x_h = nc.dram_tensor("x", [T, D], F32, kind="ExternalInput").ap()
    wq_h = nc.dram_tensor("wq", [D, D], F32, kind="ExternalInput").ap()
    wk_h = nc.dram_tensor("wk", [D, D], F32, kind="ExternalInput").ap()
    wv_h = nc.dram_tensor("wv", [D, D], F32, kind="ExternalInput").ap()
    wo_h = nc.dram_tensor("wo", [D, D], F32, kind="ExternalInput").ap()
    bo_h = nc.dram_tensor("bo", [1, D], F32, kind="ExternalInput").ap()
    w1_h = nc.dram_tensor("w1", [FF, D], F32, kind="ExternalInput").ap()
    b1_h = nc.dram_tensor("b1", [1, FF], F32, kind="ExternalInput").ap()
    w2_h = nc.dram_tensor("w2", [D, FF], F32, kind="ExternalInput").ap()
    b2_h = nc.dram_tensor("b2", [1, D], F32, kind="ExternalInput").ap()
    ident_h = nc.dram_tensor("ident", [128, 128], F32, kind="ExternalInput").ap()
    identb_h = nc.dram_tensor("identb", [128, 128], BF16, kind="ExternalInput").ap()
    ones_h = nc.dram_tensor("ones", [128, 128], F32, kind="ExternalInput").ap()
    eshift_h = nc.dram_tensor("eshift", [64, 128], BF16, kind="ExternalInput").ap()
    out_h = nc.dram_tensor("out", [T, D], F32, kind="ExternalOutput").ap()

    wkT_s = nc.dram_tensor("wkT_s", [NJ, 128, NJ, 128], BF16, kind="Internal").ap()
    wqT_s = nc.dram_tensor("wqT_s", [NJ, 128, NJ, 128], BF16, kind="Internal").ap()
    woT_s = nc.dram_tensor("woT_s", [NJ, 128, NJ, 128], BF16, kind="Internal").ap()
    w1t_s = nc.dram_tensor("w1t_s", [FF, D], BF16, kind="Internal").ap()
    w1tT_s = nc.dram_tensor("w1tT_s", [D, FF], BF16, kind="Internal").ap()
    w2t_s = nc.dram_tensor("w2t_s", [D, FF], BF16, kind="Internal").ap()
    w2T2_s = nc.dram_tensor("w2T2_s", [FF, D], BF16, kind="Internal").ap()

    stack = ExitStack()
    with stack:
        A = Arena(nc, stack, 204 * 1024)
        P = Plan(nc, stack)
        sT, sV, sS = P.sT, P.sV, P.sS

        pA = stack.enter_context(nc.psum_tensor("pA", [128, 1024], F32))
        pB = stack.enter_context(nc.psum_tensor("pB", [128, 1024], F32))
        pC = stack.enter_context(nc.psum_tensor("pC", [128, 1024], F32))
        pD = stack.enter_context(nc.psum_tensor("pD", [128, 1024], F32))

        off = [0]
        def alloc(nbytes):
            o = off[0]
            off[0] += (nbytes + 63) // 64 * 64
            return o
        o_identb = alloc(128 * 2)
        o_ident = alloc(128 * 4)
        o_ones = alloc(128 * 4)
        o_eshift = alloc(128 * 2)
        o_rows = alloc(4096)
        o_b1c = alloc(32 * 4)
        o_thr = alloc(4 * 4)
        o_onesb = alloc(128 * 2)
        o_stats = alloc(128 * 4)
        o_x = alloc(2 * 1024 * 4)
        o_M1 = alloc(97 * 1024)
        o_M2 = alloc(78 * 1024)
        o_M3 = alloc(14 * 1024)
        assert off[0] <= A.nbytes, off[0]

        identb = A.view(o_identb, 128, BF16)
        ident = A.view(o_ident, 128, F32)
        ones = A.view(o_ones, 128, F32)
        eshift = A.view(o_eshift, 128, BF16)
        bo_row = A.view(o_rows, 1024, F32)
        b2_row = A.view(o_rows, 1024, F32)
        b1c = A.view(o_b1c, 32, F32)
        thrc = A.view(o_thr, 4, F32)
        onesb = A.view(o_onesb, 128, BF16)
        stats = A.view(o_stats, 128, F32)
        xst = A.view(o_x, 2048, F32)

        kT = A.view(o_M1, NJ * 2048, BF16)
        vp = A.view(o_M1 + 32 * 1024, 16 * 1040, BF16)
        o_qa = o_M1 + 32 * 1024 + (16 * 1040 * 2 + 63) // 64 * 64
        qa = A.view(o_qa, NJ * 2048, BF16)
        woT = A.view(o_M1, NJ * 1024, BF16)
        G3 = A.view(o_M1, NF * 512, BF16)
        y3T = A.view(o_M1 + 32 * 1024, NJ * 2048, BF16)
        o_p3 = o_qa
        w1slab = A.view(o_p3, 2 * 8 * 256, BF16)
        w2buf = A.view(o_p3 + 8 * 1024, 2 * 1024, BF16)
        a2 = A.view(o_p3 + 12 * 1024, 4 * 1024, F32)
        assert o_p3 + 28 * 1024 <= o_M1 + 97 * 1024

        y1T = A.view(o_M2, NJ * 2048, BF16)
        wvT = A.view(o_M2 + 32 * 1024, NJ * 1024, BF16)
        wstage = A.view(o_M2 + 48 * 1024, 4 * 1024, BF16)
        p1buf = A.view(o_M2 + 56 * 1024, 4 * 1024, BF16)
        evactmp = A.view(o_M2 + 64 * 1024, 1024, BF16)
        wslab = A.view(o_M2 + 66 * 1024, 2 * 8 * 128, BF16)
        b1v = A.view(o_M2 + 70 * 1024, 128, F32)
        X1 = A.view(o_M2, NT * 1024, F32)
        o_sp = o_M2 + 64 * 1024
        p2chunk = A.view(o_sp, 2 * 1024, F32)
        p2g = A.view(o_sp + 8 * 1024, 1024, BF16)
        p2l = A.view(o_sp + 10 * 1024, 1024, BF16)
        p2t = A.view(o_sp + 12 * 1024, 1024, BF16)
        convt = A.view(o_sp, 4096, BF16)

        e_b = A.view(o_M2 + 32 * 1024, 16 * 1024, BF16)
        y1buf = A.view(o_M3, 2 * 1024, BF16)
        num_sb = A.view(o_M3, 1024, F32)
        num_sb1 = A.view(o_M3 + 4 * 1024, 1024, F32)
        dbuf = A.view(o_M3 + 4 * 1024, 1024, F32)
        rowbuf = A.view(o_M3 + 8 * 1024, 1024, F32)
        denrow = A.view(o_x, 2048, F32)
        rb_hi = A.view(o_M3, 2048, BF16)
        rb_lo = A.view(o_M3 + 4 * 1024, 2048, BF16)
        bo_full = A.view(o_M3 + 8 * 1024, 1024, F32)
        b2_full = A.view(o_sp, 1024, F32)
        attn_tmp = A.view(o_M3 + 12 * 1024, 1024, BF16)
        sqbuf = A.view(o_M3 + 12 * 1024, 1024, BF16)

        def kT_c(j):
            return kT[:, j * 2048:(j + 1) * 2048]
        def qa_c(j):
            return qa[:, j * 2048:(j + 1) * 2048]
        def y1T_c(j):
            return y1T[:, j * 2048:(j + 1) * 2048]
        def y3T_c(j):
            return y3T[:, j * 2048:(j + 1) * 2048]
        def wvT_c(j):
            return wvT[:, j * 1024:(j + 1) * 1024]
        def woT_c(j):
            return woT[:, j * 1024:(j + 1) * 1024]
        def G_c(f):
            return G3[:, f * 512:(f + 1) * 512]
        def xst_t(i):
            return xst[:, i * 1024:(i + 1) * 1024]
        def a2_t(i):
            return a2[:, i * 1024:(i + 1) * 1024]
        def eb(hh, s):
            i = hh * 8 + (s % 8)
            return e_b[:, i * 1024:(i + 1) * 1024]
        def ybuf(i):
            return y1buf[:, (i % 2) * 1024:((i % 2) + 1) * 1024]
        def X1_t(tt):
            return X1[:, tt * 1024:(tt + 1) * 1024]
        def vp_head(s, h):
            return vp[:, s * 1040 + 65 * h: s * 1040 + 65 * h + 65]
        def stc(i):
            return stats[:, i:i + 1]

        ST_S1, ST_S2, ST_MEAN, ST_RSTD, ST_NMR, ST_LN, ST_ABS, ST_EPS, ST_TMP = \
            0, 1, 2, 3, 4, 5, 6, 7, 8
        ST_BC = 9

        def dsem(name):
            return Sem(stack, nc, name)
        sCst = dsem("sCst")
        sStage = dsem("sStage")
        sPrepS = dsem("sPrepS")
        sXa = [dsem("sXa0"), dsem("sXa1")]
        sKQ = [dsem("sKQ0"), dsem("sKQ1")]
        sP1L = [dsem("sP1L0"), dsem("sP1L1"), dsem("sP1L2"), dsem("sP1L3")]
        sP2L = [dsem("sP2L0"), dsem("sP2L1")]
        sP2S = dsem("sP2S")
        sCnvL = dsem("sCnvL")
        sCnvS = dsem("sCnvS")
        sWoL = dsem("sWoL")
        sX2 = [dsem("sX20"), dsem("sX21")]
        sW1 = [dsem("sW10"), dsem("sW11")]
        sW2 = [dsem("sW20"), dsem("sW21")]
        sO = [dsem("sO0"), dsem("sO1")]

        def vjunk():
            P.op("vector", lambda e: e.tensor_copy(
                out=sqbuf[:, 768:1024], in_=sqbuf[:, 512:768]))

        def vjunk_tk():
            return P.op("vector", lambda e: e.tensor_copy(
                out=sqbuf[:, 768:1024], in_=sqbuf[:, 512:768]), sV)

        stats_guard = [0]

        def ln_stats(src_ap, tk_src=None, src_sem=None):
            if tk_src is not None:
                P.wait("scalar", src_sem, tk_src)
            P.wait("scalar", sV, stats_guard[0])
            P.op("scalar", lambda e: e.activation(
                out=sqbuf[:], in_=src_ap, func=AF.Copy, accum_out=stc(ST_S1)))
            P.op("scalar", lambda e: e.activation(
                out=sqbuf[:], in_=src_ap, func=AF.Square, accum_out=stc(ST_S2)))
            tk_s12 = P.op("scalar", lambda e: e.activation(
                out=stats[:, 44:48], in_=stats[:, 44:48], func=AF.Copy), sS)
            P.wait("vector", sS, tk_s12)
            P.op("vector", lambda e: e.tensor_scalar(
                out=stc(ST_LN), in0=stc(ST_S1), scalar1=stc(ST_S1), scalar2=-1.0 / (D * D),
                op0=ALU.mult, op1=ALU.mult))
            stats_guard[0] = P.op("vector", lambda e: e.tensor_scalar(
                out=stc(ST_MEAN), in0=stc(ST_S1), scalar1=1.0 / D, scalar2=None,
                op0=ALU.mult), sV)
            vjunk()
            P.op("vector", lambda e: e.tensor_scalar(
                out=stc(ST_BC), in0=stc(ST_LN), scalar1=EPS_LN, scalar2=None,
                op0=ALU.add))
            vjunk()
            tkb = vjunk_tk()
            P.wait("scalar", sV, tkb)
            P.op("scalar", lambda e: (
                e.activation(out=stc(ST_LN), in_=stc(ST_S2), func=AF.Ln,
                             scale=1.0 / D, bias=stc(ST_BC)),
                e.activation(out=stc(ST_RSTD), in_=stc(ST_LN), func=AF.Exp,
                             scale=-0.5))[-1])
            tk_r = P.op("scalar", lambda e: e.activation(
                out=stats[:, 44:48], in_=stats[:, 44:48], func=AF.Copy), sS)
            P.wait("vector", sS, tk_r)
            P.op("vector", lambda e: e.tensor_scalar(
                out=stc(ST_NMR), in0=stc(ST_MEAN), scalar1=stc(ST_RSTD), scalar2=-1.0,
                op0=ALU.mult, op1=ALU.mult))
            vjunk()
            vjunk()
            return tk_r

        P.op("sync", lambda e: e.dma_start(out=identb[:], in_=identb_h), sCst, 16)
        P.op("sync", lambda e: e.dma_start(out=ident[:], in_=ident_h), sCst, 16)
        P.op("sync", lambda e: e.dma_start(out=ones[:], in_=ones_h), sCst, 16)
        P.op("sync", lambda e: e.dma_start(out=eshift[0:64, :], in_=eshift_h), sCst, 16)
        P.op("sync", lambda e: e.dma_start(out=bo_row[0:1, :], in_=bo_h), sCst, 16)
        P.op("sync", lambda e: e.dma_start(out=b2_row[32:33, :], in_=b2_h), sCst, 16)
        P.op("sync", lambda e: e.dma_start(
            out=b1v[0:32, :], in_=b1_h.rearrange("o (a b) -> (o a) b", b=128)), sCst, 16)
        tk_consts = sCst.n

        P.wait("tensor", sCst, tk_consts)
        tkb1 = P.op("tensor", lambda e: e.matmul(
            pD[:, 0:32], b1v[0:32, :], ident[0:32, 0:32], start=True, stop=True), sT)
        P.wait("vector", sT, tkb1)
        P.op("vector", lambda e: e.tensor_copy(out=b1c[:], in_=pD[:, 0:32]))
        P.op("vector", lambda e: e.memset(onesb[64:65, :], 1.0))
        tk_init_v = P.op("vector", lambda e: e.memset(vp[:], 1.0), sV)

        if PHASES < 1:
            return _finish(nc, stack, P, A, dbg_out, locals())

        prep_order = [("wv", wv_h, None)][:max(1, N_PREP) if N_PREP >= 2 else N_PREP]
        evac_tks = [tk_init_v, tk_init_v]
        wprep_done_v = {}
        for wname, w_h, w_scratch in prep_order:
            for half in range(2):
                P.wait("gpsimd", sV, evac_tks[-1])
                tk_st = P.op("gpsimd", lambda e, w_h=w_h, half=half: e.dma_start(
                    out=wstage[:].rearrange("p (r j) -> p r j", r=4),
                    in_=w_h.rearrange("(r p) j -> p r j", p=128)[:, half * 4:half * 4 + 4, :]),
                    sStage, 16)
                for r4 in range(4):
                    rw = half * 4 + r4
                    ps = pA if rw % 2 == 0 else pD
                    P.wait("tensor", sStage, tk_st)
                    P.wait("tensor", sV, evac_tks[-2])
                    for cw in range(NJ):
                        P.op("tensor", lambda e, r4=r4, cw=cw, ps=ps: e.matmul(
                            ps[:, cw * 128:(cw + 1) * 128],
                            wstage[:].rearrange("p (r j) -> p r j", r=4)[:, r4, cw * 128:(cw + 1) * 128],
                            identb[:], start=True, stop=True),
                            sT if cw == NJ - 1 else None)
                    tk_mm = sT.n
                    P.wait("vector", sT, tk_mm)
                    if wname == "wv":
                        dv = wvT.rearrange("p (c r i) -> p c r i", c=8, r=8)
                        tk_ev = P.op("vector", lambda e, dv=dv, rw=rw, ps=ps: e.tensor_copy(
                            out=dv[:, :, rw, :],
                            in_=ps[:].rearrange("p (c i) -> p c i", c=8)), sV)
                    else:
                        P.wait("vector", sPrepS, sPrepS.n)
                        tk_ev = P.op("vector", lambda e, ps=ps: e.tensor_copy(
                            out=evactmp[:], in_=ps[:]), sV)
                        P.wait("sync", sV, tk_ev)
                        P.op("sync", lambda e, w_scratch=w_scratch, rw=rw: e.dma_start(
                            out=w_scratch[rw],
                            in_=evactmp[:].rearrange("p (c i) -> p c i", c=8)), sPrepS, 16)
                    evac_tks.append(tk_ev)
            wprep_done_v[wname] = evac_tks[-1]

        prep_weights = [("wk", wk_h, wkT_s), ("wq", wq_h, wqT_s),
                        ("wo", wo_h, woT_s)][:max(0, N_PREP - 1)]
        prep_jobs = []
        for wi2 in range(len(prep_weights)):
            for half in range(2):
                for r4 in range(4):
                    prep_jobs.append((wi2, half, r4))
        prep_idx = [0]
        prep_half_evacs = {"cur": [], "prevtk": sV.n}
        prep_store_prev = [sPrepS.n]

        def emit_prep_job(borrow_tk, borrow_ps):
            idx = prep_idx[0]
            if idx >= len(prep_jobs):
                return borrow_tk
            prep_idx[0] += 1
            wi2, half, r4 = prep_jobs[idx]
            wname, w_h, w_scratch = prep_weights[wi2]
            if r4 == 0:
                P.wait("gpsimd", sV, prep_half_evacs["prevtk"])
                tk_st = P.op("gpsimd", lambda e, w_h=w_h, half=half: e.dma_start(
                    out=wstage[:].rearrange("p (r j) -> p r j", r=4),
                    in_=w_h.rearrange("(r p) j -> p r j", p=128)[:, half * 4:half * 4 + 4, :]),
                    sStage, 16)
                prep_half_evacs["st"] = tk_st
                prep_half_evacs["cur"] = []
            rw = half * 4 + r4
            P.wait("tensor", sStage, prep_half_evacs["st"])
            P.wait("tensor", sV, borrow_tk)
            for cw in range(NJ):
                P.op("tensor", lambda e, r4=r4, cw=cw, ps=borrow_ps: e.matmul(
                    ps[:, cw * 128:(cw + 1) * 128],
                    wstage[:].rearrange("p (r j) -> p r j", r=4)[:, r4, cw * 128:(cw + 1) * 128],
                    identb[:], start=True, stop=True),
                    sT if cw == NJ - 1 else None)
            tk_mm = sT.n
            P.wait("vector", sT, tk_mm)
            P.wait("vector", sPrepS, prep_store_prev[0])
            tk_ev = P.op("vector", lambda e, ps=borrow_ps: e.tensor_copy(
                out=evactmp[:], in_=ps[:]), sV)
            P.wait("sync", sV, tk_ev)
            P.op("sync", lambda e, w_scratch=w_scratch, rw=rw: e.dma_start(
                out=w_scratch[rw],
                in_=evactmp[:].rearrange("p (c i) -> p c i", c=8)), sPrepS, 16)
            prep_store_prev[0] = sPrepS.n
            prep_half_evacs["cur"].append(tk_ev)
            if r4 == 3:
                prep_half_evacs["prevtk"] = tk_ev
            return tk_ev

        if PHASES < 2:
            return _finish(nc, stack, P, A, dbg_out, locals())

        pass1_jobs = []
        for wi, w_h in enumerate((w1_h, w2_h)):
            flat = (w_h.rearrange("(c p) j -> c p j", p=128) if w_h.shape[0] == FF
                    else w_h.rearrange("d (c j) -> (d c) j", j=1024).rearrange(
                        "(c p) j -> c p j", p=128))
            for c in range(32):
                pass1_jobs.append((flat, c))
        p1_free_v = [0, 0, 0, 0]
        p1_idx = [0]

        def emit_pass1_job():
            idx = p1_idx[0]
            if idx >= len(pass1_jobs):
                return
            p1_idx[0] += 1
            flat, c = pass1_jobs[idx]
            buf = idx % 4
            P.wait("gpsimd", sV, p1_free_v[buf])
            tk_l = P.op("gpsimd", lambda e, flat=flat, c=c, buf=buf: e.dma_start(
                out=p1buf[:, buf * 1024:(buf + 1) * 1024], in_=flat[c]), sP1L[buf], 16)
            P.wait("vector", sP1L[buf], tk_l)
            tk_a = P.op("vector", lambda e, buf=buf, idx=idx: e.reduce_sum(
                out=stats[:, 64 + idx:65 + idx],
                in_=p1buf[:, buf * 1024:(buf + 1) * 1024],
                axis=AX.X, apply_absolute_value=True), sV)
            p1_free_v[buf] = tk_a

        x_free_v = [0, 0]
        ybuf_free_t = [0, 0]
        y1_tks = []
        vput_tks = []
        for tt in range(NT):
            buf = tt % 2
            P.wait("sync", sV, x_free_v[buf])
            tk_x = P.op("sync", lambda e, tt=tt, buf=buf: e.dma_start(
                out=xst_t(buf), in_=x_h[tt * 128:(tt + 1) * 128, :]), sXa[buf], 16)
            P.wait("vector", sXa[buf], tk_x)
            ln_stats(xst_t(buf), tk_x, sXa[buf])
            P.wait("vector", sT, ybuf_free_t[buf])
            tk_y1 = P.op("vector", lambda e, buf=buf: e.tensor_scalar(
                out=ybuf(buf), in0=xst_t(buf),
                scalar1=stc(ST_RSTD), scalar2=stc(ST_NMR), op0=ALU.mult, op1=ALU.add), sV)
            x_free_v[buf] = tk_y1
            psy = pB if tt % 2 == 0 else pC
            P.wait("tensor", sV, tk_y1)
            if tt >= 2:
                P.wait("tensor", sV, y1_tks[tt - 2])
            for c in range(NJ):
                P.op("tensor", lambda e, c=c, buf=buf, psy=psy: e.matmul(
                    psy[:, c * 128:(c + 1) * 128],
                    ybuf(buf)[:, c * 128:(c + 1) * 128],
                    identb[:], start=True, stop=True),
                    sT if c == NJ - 1 else None)
            ybuf_free_t[buf] = sT.n
            P.wait("vector", sT, sT.n)
            tk_ev = P.op("vector", lambda e, tt=tt, psy=psy: e.tensor_copy(
                out=y1T[:].rearrange("p (c r) -> p c r", c=NJ)[:, :, tt * 128:(tt + 1) * 128],
                in_=psy[:].rearrange("p (c r) -> p c r", c=NJ)), sV)
            y1_tks.append(tk_ev)
            psv = pA if tt % 2 == 0 else pD
            P.wait("tensor", sV, tk_ev)
            if tt >= 2:
                P.wait("tensor", sV, vput_tks[tt - 2])
            if tt == 0:
                P.wait("tensor", sV, wprep_done_v["wv"])
            for j in range(NJ):
                for nh in range(2):
                    P.op("tensor", lambda e, j=j, nh=nh, tt=tt, psv=psv: e.matmul(
                        psv[:, nh * 512:(nh + 1) * 512],
                        y1T_c(j)[:, tt * 128:(tt + 1) * 128],
                        wvT_c(j)[:, nh * 512:(nh + 1) * 512],
                        start=(j == 0), stop=(j == NJ - 1)),
                        sT if (j == NJ - 1 and nh == 1) else None)
            P.wait("vector", sT, sT.n)
            tk_vev = P.op("vector", lambda e, tt=tt, psv=psv: e.tensor_copy(
                out=vp[:, tt * 1040:(tt + 1) * 1040].rearrange(
                    "p (h q) -> p h q", h=16)[:, :, 0:64],
                in_=psv[:].rearrange("p (h q) -> p h q", h=16)), sV)
            vput_tks.append(tk_vev)
            y1_tks[tt] = emit_prep_job(y1_tks[tt], psy)
            if tt >= 1:
                psy_o = pC if tt % 2 == 0 else pB
                y1_tks[tt - 1] = emit_prep_job(y1_tks[tt - 1], psy_o)
            for _ in range(4):
                emit_pass1_job()

        di = 0
        while prep_idx[0] < len(prep_jobs):
            tsel = NT - 1 - (di % 2)
            psel = pB if tsel % 2 == 0 else pC
            y1_tks[tsel] = emit_prep_job(y1_tks[tsel], psel)
            di += 1
        wprep_stores_all = sPrepS.n

        slab_free_v = [0, 0]
        kq_last_v = 0
        for wname, w_scr, dst_c in (("wk", wkT_s, kT_c), ("wq", wqT_s, qa_c)):
            for ic in range(NJ):
                buf = ic % 2
                P.wait("sync", sV, slab_free_v[buf])
                P.wait("sync", sPrepS, wprep_stores_all)
                tk_sl = P.op("sync", lambda e, w_scr=w_scr, ic=ic, buf=buf: e.dma_start(
                    out=wslab[:, buf * 1024:(buf + 1) * 1024].rearrange("p (c i) -> p c i", c=8),
                    in_=w_scr[ic]), sKQ[buf], 16)
                P.wait("tensor", sKQ[buf], tk_sl)
                P.wait("tensor", sV, kq_last_v)
                for j in range(NJ):
                    for qc in range(2):
                        ps = pB if qc == 0 else pC
                        for nh in range(2):
                            P.op("tensor", lambda e, j=j, qc=qc, nh=nh, buf=buf, ps=ps: e.matmul(
                                ps[:, nh * 512:(nh + 1) * 512],
                                wslab[:, buf * 1024 + j * 128:buf * 1024 + (j + 1) * 128],
                                y1T_c(j)[:, qc * 1024 + nh * 512:qc * 1024 + (nh + 1) * 512],
                                start=(j == 0), stop=(j == NJ - 1)),
                                sT if (j == NJ - 1 and qc == 1 and nh == 1) else None)
                P.wait("vector", sT, sT.n)
                P.op("vector", lambda e, dst_c=dst_c, ic=ic: e.tensor_copy(
                    out=dst_c(ic)[:, 0:1024], in_=pB[:]))
                tk_e2 = P.op("vector", lambda e, dst_c=dst_c, ic=ic: e.tensor_copy(
                    out=dst_c(ic)[:, 1024:2048], in_=pC[:]), sV)
                slab_free_v[buf] = tk_e2
                kq_last_v = tk_e2

        while p1_idx[0] < len(pass1_jobs):
            emit_pass1_job()
        P.op("vector", lambda e: e.tensor_copy(out=sqbuf[:, 256:512], in_=sqbuf[:, 0:256]))
        tk_abs = P.op("vector", lambda e: e.tensor_copy(
            out=sqbuf[:, 256:512], in_=sqbuf[:, 0:256]), sV)
        P.wait("tensor", sV, tk_abs)
        tk_red = P.op("tensor", lambda e: e.matmul(
            pD[0:1, 0:64], ones[:, 0:1], stats[:, 64:128], start=True, stop=True), sT)
        P.wait("vector", sT, tk_red)
        P.op("vector", lambda e: e.tensor_copy(out=rowbuf[0:1, 64:128], in_=pD[0:1, 0:64]))
        vjunk()
        vjunk()
        P.op("vector", lambda e: e.reduce_sum(
            out=rowbuf[0:1, 128:129], in_=rowbuf[0:1, 64:96], axis=AX.X))
        P.op("vector", lambda e: e.reduce_sum(
            out=rowbuf[0:1, 129:130], in_=rowbuf[0:1, 96:128], axis=AX.X))
        vjunk()
        vjunk()
        P.op("vector", lambda e: e.tensor_scalar(
            out=rowbuf[0:1, 0:1], in0=rowbuf[0:1, 128:129], scalar1=0.5 / (FF * D),
            scalar2=0.5e-8, op0=ALU.mult, op1=ALU.max))
        P.op("vector", lambda e: e.tensor_scalar(
            out=rowbuf[0:1, 1:2], in0=rowbuf[0:1, 128:129], scalar1=-0.5 / (FF * D),
            scalar2=-0.5e-8, op0=ALU.mult, op1=ALU.min))
        P.op("vector", lambda e: e.tensor_scalar(
            out=rowbuf[0:1, 2:3], in0=rowbuf[0:1, 129:130], scalar1=0.5 / (FF * D),
            scalar2=0.5e-8, op0=ALU.mult, op1=ALU.max))
        P.op("vector", lambda e: e.tensor_scalar(
            out=rowbuf[0:1, 3:4], in0=rowbuf[0:1, 129:130], scalar1=-0.5 / (FF * D),
            scalar2=-0.5e-8, op0=ALU.mult, op1=ALU.min))
        vjunk()
        tk_thr = vjunk_tk()
        P.wait("tensor", sV, tk_thr)
        for ci in range(4):
            P.op("tensor", lambda e, ci=ci: e.matmul(
                pD[:, ci * 16:ci * 16 + 1], ones[0:1, :], rowbuf[0:1, ci:ci + 1],
                start=(ci == 0), stop=(ci == 3)),
                sT if ci == 3 else None)
        P.wait("vector", sT, sT.n)
        tk_thc = P.op("vector", lambda e: e.tensor_copy(
            out=thrc[:], in_=pD[:, 0:64].rearrange("p (a b) -> p a b", b=16)[:, :, 0]), sV)

        if PHASES < 3:
            return _finish(nc, stack, P, A, dbg_out, locals())

        pass2_jobs = []
        for wi, (w_h, w_scr) in enumerate(((w1_h, w1t_s), (w2_h, w2t_s))):
            src = (w_h.rearrange("(c p) j -> c p j", p=128) if w_h.shape[0] == FF
                   else w_h.rearrange("d (c j) -> (d c) j", j=1024).rearrange(
                       "(c p) j -> c p j", p=128))
            dst = (w_scr.rearrange("(c p) j -> c p j", p=128) if w_scr.shape[0] == FF
                   else w_scr.rearrange("d (c j) -> (d c) j", j=1024).rearrange(
                       "(c p) j -> c p j", p=128))
            for c in range(32):
                pass2_jobs.append((src, dst, c, wi))
        p2_free_v = [kq_last_v, kq_last_v]
        p2_prev_store = [0]

        def emit_pass2_job(idx):
            src, dst, c, wi = pass2_jobs[idx]
            buf = idx % 2
            P.wait("sync", sV, p2_free_v[buf])
            tk_l = P.op("sync", lambda e, src=src, c=c, buf=buf: e.dma_start(
                out=p2chunk[:, buf * 1024:(buf + 1) * 1024], in_=src[c]), sP2L[buf], 16)
            P.wait("vector", sP2L[buf], tk_l)
            if idx == 0:
                P.wait("vector", sV, tk_thc)
            cv = p2chunk[:, buf * 1024:(buf + 1) * 1024]
            P.op("vector", lambda e, cv=cv, wi=wi: e.tensor_scalar(
                out=p2g[:], in0=cv, scalar1=thrc[:, 2 * wi:2 * wi + 1], scalar2=None,
                op0=ALU.is_gt))
            P.op("vector", lambda e, cv=cv, wi=wi: e.tensor_scalar(
                out=p2l[:], in0=cv, scalar1=thrc[:, 2 * wi + 1:2 * wi + 2], scalar2=None,
                op0=ALU.is_lt))
            P.wait("vector", sP2S, p2_prev_store[0])
            tk_t = P.op("vector", lambda e: e.tensor_sub(out=p2t[:], in0=p2g[:], in1=p2l[:]), sV)
            p2_free_v[buf] = tk_t
            P.wait("gpsimd", sV, tk_t)
            P.op("gpsimd", lambda e, dst=dst, c=c: e.dma_start(out=dst[c], in_=p2t[:]), sP2S, 16)
            p2_prev_store[0] = sP2S.n

        P.wait("tensor", sV, kq_last_v)
        av_free_v = 0
        last_exp = {}
        pass2_idx = 0
        tail_rest = [None]
        groups = [(hp, qh) for hp in range(H // 2) for qh in range(2)]

        def make_tail_rest(hp, qh, emit_recip, tk_nm):
            jc = hp
            qb = qh * 1024
            def stage(stage_i):
                if stage_i == -1:
                    stage.tk_rc = emit_recip()
                    return
                if stage_i == 0:
                    P.wait("tensor", sV, stage.tk_rc)
                    for hh in range(2):
                        psb = pC if hh == 0 else pD
                        for nh in range(2):
                            c0 = hh * 1024 + nh * 512
                            P.op("tensor", lambda e, psb=psb, nh=nh, c0=c0: e.matmul(
                                psb[:, nh * 512:(nh + 1) * 512], onesb[64:65, :],
                                rb_hi[64:65, c0:c0 + 512], start=True, stop=False))
                            P.op("tensor", lambda e, psb=psb, nh=nh, c0=c0: e.matmul(
                                psb[:, nh * 512:(nh + 1) * 512], onesb[64:65, :],
                                rb_lo[64:65, c0:c0 + 512], start=False, stop=True),
                                sT if (hh == 1 and nh == 1) else None)
                    tk_rb = sT.n
                    P.wait("vector", sT, tk_rb)
                    stage.tk_div0 = P.op("vector", lambda e, jc=jc, qb=qb: e.tensor_tensor(
                        out=qa_c(jc)[0:64, qb:qb + 1024], in0=num_sb[0:64, :],
                        in1=pC[0:64, :], op=ALU.mult), sV)
                    stage.tk_div1 = P.op("vector", lambda e: e.tensor_tensor(
                        out=attn_tmp[0:64, :], in0=num_sb1[0:64, :],
                        in1=pD[0:64, :], op=ALU.mult), sV)
                else:
                    P.wait("tensor", sV, stage.tk_div1)
                    for nh in range(2):
                        P.op("tensor", lambda e, nh=nh: e.matmul(
                            pD[:, nh * 512:(nh + 1) * 512], eshift[0:64, :],
                            attn_tmp[0:64, nh * 512:(nh + 1) * 512], start=True, stop=True),
                            sT if nh == 1 else None)
                    P.wait("vector", sT, sT.n)
                    stage.tk_put = P.op("vector", lambda e, jc=jc, qb=qb: e.tensor_copy(
                        out=qa_c(jc)[64:128, qb:qb + 1024], in_=pD[64:128, :]), sV)
            stage.tk_div0 = None
            stage.tk_div1 = None
            stage.tk_put = None
            return stage

        for g, (hp, qh) in enumerate(groups):
            jc = hp
            h0, h1 = 2 * hp, 2 * hp + 1
            qb = qh * 1024
            exp_tks = {}

            def emit_scores(s, jc=jc, qb=qb, exp_tks=exp_tks):
                for hh in range(2):
                    po = 64 * hh
                    ps = pA if hh == 0 else pB
                    if last_exp.get(hh):
                        P.wait("tensor", sS, last_exp[hh])
                    for nh in range(2):
                        P.op("tensor", lambda e, s=s, nh=nh, hh=hh, ps=ps, jc=jc, po=po, qb=qb: e.matmul(
                            ps[:, nh * 512:(nh + 1) * 512],
                            kT_c(jc)[po:po + 64, s * 128:(s + 1) * 128],
                            qa_c(jc)[po:po + 64, qb + nh * 512:qb + (nh + 1) * 512],
                            start=True, stop=True),
                            sT if nh == 1 else None)
                    P.wait("scalar", sT, sT.n)
                    tk_e = P.op("scalar", lambda e, s=s, hh=hh, ps=ps: e.activation(
                        out=eb(hh, s), in_=ps[:], func=AF.Exp, scale=0.125), sS)
                    last_exp[hh] = tk_e
                    exp_tks[(hh, s)] = tk_e

            def emit_av(s, hp=hp, exp_tks=exp_tks):
                for hh in range(2):
                    h = 2 * hp + hh
                    psav = pC if hh == 0 else pD
                    P.wait("tensor", sS, exp_tks[(hh, s)])
                    if s == 0:
                        P.wait("tensor", sV, av_free_v)
                    for nh in range(2):
                        P.op("tensor", lambda e, s=s, h=h, nh=nh, psav=psav: e.matmul(
                            psav[0:65, nh * 512:(nh + 1) * 512],
                            vp_head(s, h),
                            eb(h % 2, s)[:, nh * 512:(nh + 1) * 512],
                            start=(s == 0), stop=(s == NT - 1)),
                            sT if (s == NT - 1 and nh == 1) else None)

            emit_scores(0)
            if tail_rest[0] is not None:
                tail_rest[0](-1)
            for s in range(1, 6):
                emit_scores(s)
            if tail_rest[0] is not None:
                tail_rest[0](0)
            emit_scores(6)
            if tail_rest[0] is not None:
                tail_rest[0](1)
                av_free_v = tail_rest[0].tk_put
                tail_rest[0] = None
            for _ in range(4):
                if pass2_idx < len(pass2_jobs):
                    emit_pass2_job(pass2_idx)
                    pass2_idx += 1
            for s in range(7, NT):
                emit_scores(s)
                emit_av(s - 7)
            for s in range(NT - 7, NT):
                emit_av(s)
            tk_av = sT.n
            P.wait("vector", sT, tk_av)
            P.op("vector", lambda e: e.tensor_scalar(
                out=denrow[64:65, 0:1024], in0=pC[64:65, 0:1024], scalar1=EPS_ATT,
                scalar2=None, op0=ALU.add))
            tk_dn = P.op("vector", lambda e: e.tensor_scalar(
                out=denrow[64:65, 1024:2048], in0=pD[64:65, 0:1024], scalar1=EPS_ATT,
                scalar2=None, op0=ALU.add), sV)
            P.op("vector", lambda e: e.tensor_copy(
                out=num_sb[0:64, :], in_=pC[0:64, 0:1024]))
            tk_nm = P.op("vector", lambda e: e.tensor_copy(
                out=num_sb1[0:64, :], in_=pD[0:64, 0:1024]), sV)
            def emit_recip(tk_dn=tk_dn):
                P.wait("scalar", sV, tk_dn)
                P.op("scalar", lambda e: e.activation(
                    out=denrow[32:33, 0:2048], in_=denrow[64:65, 0:2048], func=AF.Ln))
                tk_e = P.op("scalar", lambda e: e.activation(
                    out=denrow[64:65, 0:2048], in_=denrow[32:33, 0:2048], func=AF.Exp,
                    scale=-1.0), sS)
                P.wait("vector", sS, tk_e)
                P.op("vector", lambda e: e.tensor_copy(
                    out=rb_hi[64:65, 0:2048], in_=denrow[64:65, 0:2048]))
                return P.op("vector", lambda e: e.tensor_tensor(
                    out=rb_lo[64:65, 0:2048], in0=denrow[64:65, 0:2048],
                    in1=rb_hi[64:65, 0:2048], op=ALU.subtract), sV)
            tail_rest[0] = make_tail_rest(hp, qh, emit_recip, tk_nm)
            if not DEFER_TAIL:
                tail_rest[0](0)
                tail_rest[0](1)
                av_free_v = tail_rest[0].tk_put
                tail_rest[0] = None
        if tail_rest[0] is not None:
            tail_rest[0](-1)
            tail_rest[0](0)
            tail_rest[0](1)
            tail_rest[0] = None
        while pass2_idx < len(pass2_jobs):
            emit_pass2_job(pass2_idx)
            pass2_idx += 1
        attn_done_t = sT.n


        if PHASES < 4:
            return _finish(nc, stack, P, A, dbg_out, locals())

        P.wait("sync", sT, attn_done_t)
        P.wait("sync", sPrepS, wprep_stores_all)
        for j in range(NJ):
            P.op("sync", lambda e, j=j: e.dma_start(
                out=woT_c(j).rearrange("p (r i) -> p r i", r=8),
                in_=woT_s[:, :, j, :].rearrange("r p i -> p r i")), sWoL, 16)
        tk_wo = sWoL.n

        P.wait("sync", sP2S, sP2S.n)
        conv_free = 0
        for j in range(NJ):
            P.wait("sync", sCnvS, conv_free)
            tk_c = P.op("sync", lambda e, j=j: e.dma_start(
                out=convt[:], in_=w1t_s[:, j * 128:(j + 1) * 128], transpose=True), sCnvL, 16)
            P.wait("sync", sCnvL, tk_c)
            P.op("sync", lambda e, j=j: e.dma_start(
                out=w1tT_s[j * 128:(j + 1) * 128, :], in_=convt[:]), sCnvS, 16)
            conv_free = sCnvS.n
        NW2 = FF // 128
        LOOK = 3
        tkT2 = [0] * NW2
        tkS2 = [0] * NW2
        for i in range(NW2 + LOOK):
            if i < NW2:
                sl = i % 4
                if i >= 4:
                    P.wait("sync", sCnvS, tkS2[i - 4])
                tkT2[i] = P.op("sync", lambda e, f=i, sl=sl: e.dma_start(
                    out=convt[:, sl * 1024:(sl + 1) * 1024],
                    in_=w2t_s[:, f * 128:(f + 1) * 128], transpose=True), sCnvL, 16)
            k = i - LOOK
            if 0 <= k < NW2:
                P.wait("sync", sCnvL, tkT2[k])
                tkS2[k] = P.op("sync", lambda e, f=k, sl=k % 4: e.dma_start(
                    out=w2T2_s[f * 128:(f + 1) * 128, :],
                    in_=convt[:, sl * 1024:(sl + 1) * 1024]), sCnvS, 16)
        conv_free = sCnvS.n

        attn_v_done = sV.n
        x1_tks = []
        P.wait("tensor", sV, attn_v_done)
        for nh in range(2):
            P.op("tensor", lambda e, nh=nh: e.matmul(
                pC[:, nh * 512:(nh + 1) * 512], ones[0:1, :],
                bo_row[0:1, nh * 512:(nh + 1) * 512], start=True, stop=True),
                sT if nh == 1 else None)
        P.wait("vector", sT, sT.n)
        tk_bo_v = P.op("vector", lambda e: e.tensor_copy(out=bo_full[:], in_=pC[:]), sV)
        x2_free = [attn_v_done, attn_v_done]
        ln2_ps_free = {}
        wo_last_mm = 0
        P.wait("tensor", sWoL, tk_wo)
        for tt in range(NT):
            buf = tt % 2
            P.wait("gpsimd", sV, x2_free[buf])
            tk_x = P.op("gpsimd", lambda e, tt=tt, buf=buf: e.dma_start(
                out=xst_t(buf), in_=x_h[tt * 128:(tt + 1) * 128, :]), sX2[buf], 16)
            ps = (pA, pB, pC, pD)[tt % 4]
            if tt == 2:
                P.wait("tensor", sV, tk_bo_v)
            if tt >= 4:
                P.wait("tensor", sV, ln2_ps_free[tt - 4])
            for j in range(NJ):
                for nh in range(2):
                    P.op("tensor", lambda e, j=j, nh=nh, tt=tt, ps=ps: e.matmul(
                        ps[:, nh * 512:(nh + 1) * 512],
                        qa_c(j)[:, tt * 128:(tt + 1) * 128],
                        woT_c(j)[:, nh * 512:(nh + 1) * 512],
                        start=(j == 0), stop=(j == NJ - 1)),
                        sT if (j == NJ - 1 and nh == 1) else None)
            tk_wmm = sT.n
            wo_last_mm = tk_wmm
            P.wait("vector", sT, tk_wmm)
            tk_d = P.op("vector", lambda e, ps=ps: e.tensor_tensor(
                out=dbuf[:], in0=ps[:], in1=bo_full[:], op=ALU.add), sV)
            ln2_ps_free[tt] = tk_d
            ln_stats(dbuf[:], tk_d, sV)
            P.op("vector", lambda e: e.tensor_scalar(
                out=dbuf[:], in0=dbuf[:], scalar1=stc(ST_RSTD), scalar2=stc(ST_NMR),
                op0=ALU.mult, op1=ALU.add))
            P.wait("vector", sX2[buf], tk_x)
            tk_x1 = P.op("vector", lambda e, tt=tt, buf=buf: e.tensor_tensor(
                out=X1_t(tt), in0=dbuf[:], in1=xst_t(buf), op=ALU.add), sV)
            x2_free[buf] = tk_x1
            x1_tks.append(tk_x1)
        x1_done_v = sV.n

        if PHASES < 5:
            return _finish(nc, stack, P, A, dbg_out, locals())

        y3_tks = []
        y3buf_free_t = [0, 0]
        for tt in range(NT):
            ln_stats(X1_t(tt), x1_tks[tt], sV)
            P.wait("vector", sT, y3buf_free_t[tt % 2])
            tk_y3 = P.op("vector", lambda e, tt=tt: e.tensor_scalar(
                out=ybuf(tt), in0=X1_t(tt), scalar1=stc(ST_RSTD), scalar2=stc(ST_NMR),
                op0=ALU.mult, op1=ALU.add), sV)
            psy = pC if tt % 2 == 0 else pD
            P.wait("tensor", sV, tk_y3)
            if tt >= 2:
                P.wait("tensor", sV, y3_tks[tt - 2])
            for c in range(NJ):
                P.op("tensor", lambda e, c=c, tt=tt, psy=psy: e.matmul(
                    psy[:, c * 128:(c + 1) * 128],
                    ybuf(tt)[:, c * 128:(c + 1) * 128],
                    identb[:], start=True, stop=True),
                    sT if c == NJ - 1 else None)
            y3buf_free_t[tt % 2] = sT.n
            P.wait("vector", sT, sT.n)
            tk_ev = P.op("vector", lambda e, tt=tt, psy=psy: e.tensor_copy(
                out=y3T[:].rearrange("p (c r) -> p c r", c=NJ)[:, :, tt * 128:(tt + 1) * 128],
                in_=psy[:].rearrange("p (c r) -> p c r", c=NJ)), sV)
            y3_tks.append(tk_ev)

        P.wait("tensor", sV, y3_tks[-2])
        for nh in range(2):
            P.op("tensor", lambda e, nh=nh: e.matmul(
                pC[:, nh * 512:(nh + 1) * 512], ones[32:33, :],
                b2_row[32:33, nh * 512:(nh + 1) * 512], start=True, stop=True),
                sT if nh == 1 else None)
        tk_b2mm = sT.n
        P.wait("vector", sT, tk_b2mm)
        P.wait("vector", sCnvS, conv_free)
        tk_bias_v = P.op("vector", lambda e: e.tensor_copy(out=b2_full[:], in_=pC[:]), sV)

        w1_free_t = [wo_last_mm, wo_last_mm]
        w2_free_t = [wo_last_mm, wo_last_mm]
        gelu_tks = {}
        h2_evac_v = {}
        store_cnt = 0
        hps = [pA, pB, pC, pD]
        ln4_tail = [None]

        def make_ln4_tail(qc):
            def tail():
                for tt4 in range(4):
                    P.op("vector", lambda e, tt4=tt4: e.tensor_reduce(
                        out=stats[:, 16 + tt4:17 + tt4], in_=a2_t(tt4), axis=AX.X, op=ALU.add))
                    P.op("vector", lambda e, tt4=tt4: e.scalar_tensor_tensor(
                        out=sqbuf[:], in0=a2_t(tt4), scalar=1.0, in1=a2_t(tt4),
                        op0=ALU.mult, op1=ALU.mult))
                    P.op("vector", lambda e, tt4=tt4: e.tensor_reduce(
                        out=stats[:, 20 + tt4:21 + tt4], in_=sqbuf[:], axis=AX.X, op=ALU.add))
                P.op("vector", lambda e: e.tensor_mul(
                    out=stats[:, 36:40], in0=stats[:, 16:20], in1=stats[:, 16:20]))
                P.op("vector", lambda e: e.tensor_scalar(
                    out=stats[:, 24:28], in0=stats[:, 16:20], scalar1=1.0 / D, scalar2=None,
                    op0=ALU.mult))
                vjunk()
                P.op("vector", lambda e: e.tensor_scalar(
                    out=stats[:, 28:32], in0=stats[:, 36:40], scalar1=-1.0 / (D * D),
                    scalar2=EPS_LN, op0=ALU.mult, op1=ALU.add))
                vjunk()
                tk_v4 = vjunk_tk()
                P.wait("scalar", sV, tk_v4)
                for tt4 in range(4):
                    P.op("scalar", lambda e, tt4=tt4: e.activation(
                        out=stats[:, 36 + tt4:37 + tt4], in_=stats[:, 20 + tt4:21 + tt4],
                        func=AF.Ln, scale=1.0 / D, bias=stats[:, 28 + tt4:29 + tt4]))
                P.op("scalar", lambda e: e.activation(
                    out=stats[:, 32:36], in_=stats[:, 36:40], func=AF.Exp, scale=-0.5))
                tk_r4 = P.op("scalar", lambda e: e.activation(
                    out=stats[:, 44:48], in_=stats[:, 44:48], func=AF.Copy), sS)
                P.wait("vector", sS, tk_r4)
                P.op("vector", lambda e: e.tensor_mul(
                    out=stats[:, 40:44], in0=stats[:, 24:28], in1=stats[:, 32:36]))
                vjunk()
                vjunk()
                for tt4 in range(4):
                    tt = qc * 4 + tt4
                    buf = tt % 2
                    P.wait("vector", sO[buf], sO[buf].n)
                    P.op("vector", lambda e, tt4=tt4: e.tensor_scalar(
                        out=a2_t(tt4), in0=a2_t(tt4), scalar1=stats[:, 32 + tt4:33 + tt4],
                        scalar2=stats[:, 40 + tt4:41 + tt4], op0=ALU.mult, op1=ALU.subtract))
                    tk_o = P.op("vector", lambda e, tt=tt, tt4=tt4, buf=buf: e.tensor_tensor(
                        out=xst_t(buf), in0=a2_t(tt4), in1=X1_t(tt), op=ALU.add), sV)
                    P.wait("gpsimd", sV, tk_o)
                    P.op("gpsimd", lambda e, tt=tt, buf=buf: e.dma_start(
                        out=out_h[tt * 128:(tt + 1) * 128, :], in_=xst_t(buf)), sO[buf], 16)
            return tail

        for qc in range(4):
            for fs in range(16):
                buf = fs % 2
                P.wait("sync", sT, w1_free_t[buf])
                tk_sl = P.op("sync", lambda e, fs=fs, buf=buf: e.dma_start(
                    out=w1slab[:, buf * 2048:(buf + 1) * 2048].rearrange("p (c f) -> p c f", c=8),
                    in_=w1tT_s.rearrange("(c p) f -> p c f", p=128)[:, :, fs * 256:(fs + 1) * 256]),
                    sW1[buf], 16)
                P.wait("tensor", sW1[buf], tk_sl)
                for fh in range(2):
                    f = fs * 2 + fh
                    psh = pA[:, (f % 2) * 512:((f % 2) + 1) * 512]
                    if f >= 2:
                        P.wait("tensor", sS, gelu_tks[(qc, f - 2)])
                    elif qc == 0:
                        P.wait("tensor", sV, y3_tks[-1])
                    else:
                        P.wait("tensor", sV, h2_evac_v[(qc - 1, 0)])
                    for j in range(NJ):
                        P.op("tensor", lambda e, j=j, fh=fh, buf=buf, qc=qc, psh=psh: e.matmul(
                            psh,
                            w1slab[:, buf * 2048 + j * 256 + fh * 128:
                                   buf * 2048 + j * 256 + (fh + 1) * 128],
                            y3T_c(j)[:, qc * 512:(qc + 1) * 512],
                            start=(j == 0), stop=(j == NJ - 1)),
                            sT if j == NJ - 1 else None)
                    tk_h = sT.n
                    if fh == 1:
                        w1_free_t[buf] = tk_h
                    P.wait("scalar", sT, tk_h)
                    gelu_tks[(qc, f)] = P.op("scalar", lambda e, f=f, psh=psh: e.activation(
                        out=G_c(f), in_=psh, func=AF.Gelu, scale=1.0, bias=b1c[:, f:f + 1]), sS)
                if fs == 5 and ln4_tail[0] is not None:
                    ln4_tail[0]()
                    ln4_tail[0] = None
            for f in range(NF):
                buf = f % 2
                P.wait("sync", sT, w2_free_t[buf])
                tk_w2 = P.op("sync", lambda e, f=f, buf=buf: e.dma_start(
                    out=w2buf[:, buf * 1024:(buf + 1) * 1024],
                    in_=w2T2_s[f * 128:(f + 1) * 128, :]), sW2[buf], 16)
                P.wait("tensor", sW2[buf], tk_w2)
                if f == 0:
                    P.wait("tensor", sS, gelu_tks[(qc, NF - 1)])
                    if qc == 0:
                        P.wait("tensor", sV, tk_bias_v)
                    if qc > 0:
                        for tt4 in range(1, 4):
                            P.wait("tensor", sV, h2_evac_v[(qc - 1, tt4)])
                for tt4 in range(4):
                    for nh in range(2):
                        P.op("tensor", lambda e, f=f, tt4=tt4, nh=nh, buf=buf: e.matmul(
                            hps[tt4][:, nh * 512:(nh + 1) * 512],
                            G_c(f)[:, tt4 * 128:(tt4 + 1) * 128],
                            w2buf[:, buf * 1024 + nh * 512:buf * 1024 + (nh + 1) * 512],
                            start=(f == 0), stop=(f == NF - 1)),
                            sT if (tt4 == 3 and nh == 1) else None)
                w2_free_t[buf] = sT.n
            tk_h2 = sT.n
            P.wait("vector", sT, tk_h2)
            for tt4 in range(4):
                h2_evac_v[(qc, tt4)] = P.op("vector", lambda e, tt4=tt4: e.tensor_tensor(
                    out=a2_t(tt4), in0=hps[tt4][:], in1=b2_full[:], op=ALU.add), sV)
            ln4_tail[0] = make_ln4_tail(qc)

        if ln4_tail[0] is not None:
            ln4_tail[0]()
            ln4_tail[0] = None
        P.wait("sync", sO[0], sO[0].n)
        P.wait("sync", sO[1], sO[1].n)
        P.wait("gpsimd", sO[0], sO[0].n)
        P.wait("gpsimd", sO[1], sO[1].n)
        return _finish(nc, stack, P, A, dbg_out, locals())


def _finish(nc, stack, P, A, dbg_out, env):
    sDbg = Sem(stack, nc, "sDbg")
    if DBG:
        views = {
            "y1T": (env.get("y1T"), PHASES >= 2), "kT": (env.get("kT"), 2 <= PHASES < 4),
            "qa": (env.get("qa"), 2 <= PHASES < 5), "vp": (env.get("vp"), 2 <= PHASES < 5),
            "thrc": (env.get("thrc"), PHASES >= 2), "X1": (env.get("X1"), PHASES >= 4),
            "wvT": (env.get("wvT"), 1 <= PHASES < 3),
        }
        for s in (P.sT, P.sV, P.sS):
            P.wait("gpsimd", s, s.n)
        for name, (v, ok) in views.items():
            if not ok or v is None:
                continue
            shape = [128, v.shape[1]]
            d = nc.dram_tensor("dbg_" + name, shape, v.dtype, kind="ExternalOutput").ap()
            P.op("gpsimd", lambda e, d=d, v=v: e.dma_start(out=d, in_=v[:]), sDbg, 16)
        P.wait("gpsimd", sDbg, sDbg.n)
    block = stack.enter_context(nc.Block())
    P.emit(block)
    return nc


_NC_CACHE = {}


def kernel(L0_attn_x, L0_attn_Wq, L0_attn_Wk, L0_attn_Wv, L0_attn_Wo, L0_attn_bo,
           L0_ffn_W1, L0_ffn_b1, L0_ffn_W2, L0_ffn_b2):
    x = np.ascontiguousarray(np.asarray(L0_attn_x, dtype=np.float32))
    wq = np.ascontiguousarray(np.asarray(L0_attn_Wq, dtype=np.float32))
    wk = np.ascontiguousarray(np.asarray(L0_attn_Wk, dtype=np.float32))
    wv = np.ascontiguousarray(np.asarray(L0_attn_Wv, dtype=np.float32))
    wo = np.ascontiguousarray(np.asarray(L0_attn_Wo, dtype=np.float32))
    bo = np.ascontiguousarray(np.asarray(L0_attn_bo, dtype=np.float32)).reshape(1, D)
    w1 = np.ascontiguousarray(np.asarray(L0_ffn_W1, dtype=np.float32))
    b1 = np.ascontiguousarray(np.asarray(L0_ffn_b1, dtype=np.float32)).reshape(1, FF)
    w2 = np.ascontiguousarray(np.asarray(L0_ffn_W2, dtype=np.float32))
    b2 = np.ascontiguousarray(np.asarray(L0_ffn_b2, dtype=np.float32)).reshape(1, D)

    ident = np.eye(128, dtype=np.float32)
    identb = np.eye(128, dtype=ml_dtypes.bfloat16)
    ones = np.ones((128, 128), dtype=np.float32)
    eshift = np.zeros((64, 128), dtype=ml_dtypes.bfloat16)
    for i in range(64):
        eshift[i, 64 + i] = 1.0

    if "nc" not in _NC_CACHE:
        _NC_CACHE["nc"] = build_program()
    nc = _NC_CACHE["nc"]

    in_maps = []
    for c in range(N_CORES):
        in_maps.append({
            "x": np.ascontiguousarray(x[c]),
            "wq": wq, "wk": wk, "wv": wv, "wo": wo, "bo": bo,
            "w1": w1, "b1": b1, "w2": w2, "b2": b2,
            "ident": ident, "identb": identb, "ones": ones,
            "eshift": eshift,
        })
    res = run_bass_kernel_spmd(nc, in_maps, core_ids=list(range(N_CORES)))
    out = np.stack([np.asarray(r["out"]) for r in res.results], axis=0)
    return out.astype(np.float32)
